# version 07
# baseline (speedup 1.0000x reference)
import numpy as np
import ml_dtypes
from contextlib import ExitStack

import concourse.bass as bass
import concourse.mybir as mybir
from concourse.bass_utils import run_bass_kernel_spmd

F32 = mybir.dt.float32
BF16 = mybir.dt.bfloat16
AF = mybir.ActivationFunctionType
ALU = mybir.AluOpType
AX = mybir.AxisListType

D = 2048
NSEQ = 2048
H = 16
HKV = 4
HD = 128
II = 5632
TQ = 1024
SCALE = float(HD) ** -0.5
C52 = 1.5 * 2.0 ** 52
C32 = 1.5 * 2.0 ** 23
EPS_LN = 1e-5
AMAX_MIN = 127.0e-8
NDC = D // 128
NIC = II // 128
NTC = TQ // 128
NKC = NSEQ // 128

ALL_SEMS = ("sv", "sa", "sp", "st", "dxc", "dl", "dso")


class Prog:
    ENGS = ("sync", "scalar", "vector", "tensor", "gpsimd")
    SELF = {"vector": "sv", "scalar": "sa", "gpsimd": "sp", "tensor": "st"}

    def __init__(self):
        self.stream = {e: [] for e in self.ENGS}
        self.cnt = {s: 0 for s in ALL_SEMS}

    def wait(self, eng, sem, val):
        if val > 0:
            self.stream[eng].append(("w", sem, val))

    def op(self, eng, fn):
        s = self.SELF[eng]
        self.stream[eng].append(("w", s, self.cnt[s]))
        self.cnt[s] += 1
        self.stream[eng].append(("i", fn, s, 1))
        return self.cnt[s]

    def traw(self, fn):
        self.stream["tensor"].append(("n", fn))

    def tclose(self, fn):
        self.cnt["st"] += 1
        self.stream["tensor"].append(("i", fn, "st", 1))
        return self.cnt["st"]

    def dma(self, eng, sem, fn):
        self.cnt[sem] += 16
        self.stream[eng].append(("i", fn, sem, 16))
        return self.cnt[sem]

    def barrier(self):
        snap = dict(self.cnt)
        for e in self.ENGS:
            own = self.SELF.get(e)
            for s in ALL_SEMS:
                if s != own:
                    self.wait(e, s, snap[s])




class SbufPool:

    def __init__(self, nc):
        self.nc = nc
        self.free = [(0, 1 << 30)]
        self.hwm = 0
        self.items = []
        self.where = {}

    def alloc(self, name, shape, dtype):
        pp = int(np.prod(shape[1:])) * mybir.dt.size(dtype)
        pp = (pp + 31) // 32 * 32
        cands = [(l, s, i) for i, (s, l) in enumerate(self.free) if l >= pp]
        if not cands:
            raise MemoryError(f"SbufPool: no slot for {name} ({pp}B)")
        l, s, i = min(cands)
        addr = s
        if l == pp:
            self.free.pop(i)
        else:
            self.free[i] = (s + pp, l - pp)
        h = self.nc.virtual_sbuf_tensor("v_" + name, shape, dtype)
        self.items.append((h, addr))
        self.where[h.name] = (addr, pp)
        self.hwm = max(self.hwm, addr + pp)
        return h

    def release(self, handles):
        for h in handles:
            addr, size = self.where.pop(h.name)
            self.free.append((addr, size))
        self.free.sort()
        merged = []
        for s, l in self.free:
            if merged and merged[-1][0] + merged[-1][1] == s:
                merged[-1][1] += l
            else:
                merged.append([s, l])
        self.free = [(s, l) for s, l in merged]

    def finalize(self, ctx):
        arena = ctx.enter_context(
            self.nc.sbuf_tensor("sbuf_pool_arena", [128, self.hwm], mybir.dt.uint8))
        base = self.nc.lookup_mloc(arena).addr
        for h, addr in self.items:
            self.nc._tensor(h.name, h.shape, h.dtype, type="SB")
            mloc = self.nc.lookup_mloc(h)
            mloc.allocated = not self.nc.target_bir_lowering
            mloc.addr = base + addr


def build_program(nc, dbg=None):
    dbg = list(dbg or ())
    p = Prog()
    ctx = ExitStack()

    xb = nc.dram_tensor("xb", [NSEQ, D], F32, kind="ExternalInput").ap()
    cosk_d = nc.dram_tensor("cosk", [128, NSEQ], F32, kind="ExternalInput").ap()
    wq_d = nc.dram_tensor("wq", [D, H * HD], F32, kind="ExternalInput").ap()
    wk_d = nc.dram_tensor("wk", [D, HKV * HD], F32, kind="ExternalInput").ap()
    wv_d = nc.dram_tensor("wv", [D, HKV * HD], F32, kind="ExternalInput").ap()
    wo_d = nc.dram_tensor("wo", [H * HD, D], F32, kind="ExternalInput").ap()
    w1_d = nc.dram_tensor("w1", [2 * II, D], F32, kind="ExternalInput").ap()
    w2_d = nc.dram_tensor("w2", [D, II], F32, kind="ExternalInput").ap()
    idb_d = nc.dram_tensor("idb", [128, 128], BF16, kind="ExternalInput").ap()
    idf_d = nc.dram_tensor("idf", [128, 128], F32, kind="ExternalInput").ap()
    onescol_d = nc.dram_tensor("onescol_b", [128, 1], BF16, kind="ExternalInput").ap()
    onesrow_f_d = nc.dram_tensor("onesrow_f", [1, 128], F32, kind="ExternalInput").ap()
    onesrow_b_d = nc.dram_tensor("onesrow_b", [1, 128], BF16, kind="ExternalInput").ap()
    out_d = nc.dram_tensor("out", [TQ, D], F32, kind="ExternalOutput").ap()

    def tap(name, ap, shape, dtype):
        if name not in dbg:
            return
        t = nc.dram_tensor("dbg_" + name, shape, dtype, kind="ExternalOutput").ap()
        p.barrier()
        p.dma("sync", "dso", lambda e, a=ap, t=t: e.dma_start(out=t, in_=a))

    pool = SbufPool(nc)
    sb = lambda n, s, d: pool.alloc(n, s, d)

    pA = ctx.enter_context(nc.psum_tensor("pA", [128, 1024], F32))
    pB = ctx.enter_context(nc.psum_tensor("pB", [128, 1024], F32))
    pT = ctx.enter_context(nc.psum_tensor("pT", [128, 2048], BF16))
    pC = ctx.enter_context(nc.psum_tensor("pC", [128, 512], F32))
    pD = ctx.enter_context(nc.psum_tensor("pD", [128, 512], F32))
    pT32 = pT[:].bitcast(F32)

    idb = sb("idb", [128, 128], BF16)
    idf = sb("idf", [128, 128], F32)
    onescol_b = sb("onescol_b", [128, 1], BF16)
    onesrow_f = sb("onesrow_f", [1, 128], F32)
    onesrow_b = sb("onesrow_b", [1, 128], BF16)
    mcol = sb("mcol", [128, 1], F32)
    epscol = sb("epscol", [128, 1], F32)
    x1 = sb("x1", [128, NTC * D], BF16)
    bias_bc = sb("bias_bc", [128, H], F32)
    colp = sb("colp", [128, 16], F32)
    qkrow = sb("qkrow", [1, 64], F32)
    scr_a = sb("scr_a", [128, D], F32)
    bnst = sb("bnst", [128, 24], F32)
    bnag = sb("bnag", [128, 2], F32)
    bnag8 = sb("bnag8", [128, 16], F32)
    lncol = sb("lncol", [128, 24], F32)
    acol8 = sb("acol8", [128, 24], F32)
    amaxf = sb("amaxf", [128, 1], F32)

    for dst, src in ((idb, idb_d), (idf, idf_d), (onescol_b, onescol_d),
                     (onesrow_f, onesrow_f_d), (onesrow_b, onesrow_b_d)):
        p.dma("sync", "dl", lambda e, d=dst, s=src: e.dma_start(out=d[:], in_=s))
    dl_consts = p.cnt["dl"]

    p.op("vector", lambda e: e.memset(mcol[:], 1.0))
    p.op("vector", lambda e: e.tensor_scalar(out=mcol[:], in0=mcol[:], scalar1=C52,
                                             scalar2=-C52, op0=ALU.add, op1=ALU.add))
    p.op("vector", lambda e: e.tensor_scalar(out=mcol[:], in0=mcol[:], scalar1=(C52 - C32),
                                             scalar2=C32, op0=ALU.mult, op1=ALU.add))
    p.op("vector", lambda e: e.memset(epscol[:], EPS_LN))

    gX = []
    sbX = lambda n, s, d: (gX.append(pool.alloc(n, s, d)) or gX[-1])
    cosk_sb = sbX("cosk_sb", [128, NSEQ], F32)
    x_fm = sbX("x_fm", [128, NDC * NSEQ], BF16)
    ring = [sbX(f"ring{i}", [128, D], BF16) for i in range(2)]
    wfold = [sbX(f"wfold{i}", [128, D], BF16) for i in range(2)]
    wv_half = sbX("wv_half", [128, 8 * 512], BF16)

    p.dma("sync", "dl", lambda e: e.dma_start(out=cosk_sb[:], in_=cosk_d))
    dl_cos = p.cnt["dl"]

    tr_done = [0] * NKC
    evac_done = [0] * NKC
    for tc in range(NKC):
        slot = ring[tc % 2]
        if tc >= 2:
            p.wait("gpsimd", "st", tr_done[tc - 2])
        p.dma("gpsimd", "dxc",
              lambda e, s=slot, tc=tc: e.dma_start(out=s[:], in_=xb[tc * 128:(tc + 1) * 128, :]))
        dxc_t = p.cnt["dxc"]
        p.wait("tensor", "dxc", dxc_t)
        if tc == 0:
            p.wait("tensor", "dl", dl_consts)
        if tc >= 1:
            p.wait("tensor", "sv", evac_done[tc - 1])
        for dc in range(NDC):
            fn = (lambda e, s=slot, dc=dc:
                  e.transpose(pT[:, dc * 128:(dc + 1) * 128],
                              s[:, dc * 128:(dc + 1) * 128], idb[:]))
            if dc < NDC - 1:
                p.traw(fn)
            else:
                tr_done[tc] = p.tclose(fn)
        p.wait("vector", "st", tr_done[tc])
        src = pT[:].rearrange("q (dc t) -> q dc t", dc=NDC)
        dst = x_fm[:].rearrange("q (dc t) -> q dc t", dc=NDC)[:, :, tc * 128:(tc + 1) * 128]
        evac_done[tc] = p.op("vector", lambda e, s=src, d=dst: e.tensor_copy(out=d, in_=s))

    tap("x_fm", x_fm[:], [128, NDC * NSEQ], BF16)

    gQKV = []
    sbQ = lambda n, s, d: (gQKV.append(pool.alloc(n, s, d)) or gQKV[-1])
    q_fm = sbQ("q_fm", [128, H * TQ], BF16)
    k_fm = sbQ("k_fm", [128, HKV * NSEQ], BF16)
    v_tok = sbQ("v_tok", [128, NKC * 512], BF16)

    p.barrier()

    def proj_stream(n_cols, w_dram, out_fm, out_stride, n_tb, prev_tail,
                    prev_fold_tail=(0, 0), prev_mm_tail=(0, 0)):
        psr = [pC, pD]
        fold_done = [0] * (n_cols + 1)
        mm_done = [0] * n_cols
        ev = [0] * (n_cols * n_tb)

        def emit_load_fold(ec):
            cslot, fslot = ring[ec % 2], wfold[ec % 2]
            if ec >= 2:
                p.wait("gpsimd", "sv", fold_done[ec - 2])
            else:
                p.wait("gpsimd", "sv", prev_fold_tail[ec])
            src = w_dram[:, ec * 128:(ec + 1) * 128].rearrange("(dc q) j -> q dc j", q=128)
            dstv = cslot[:].rearrange("q (dc j) -> q dc j", dc=NDC)
            p.dma("gpsimd", "dxc", lambda e, s=src, d=dstv: e.dma_start(out=d, in_=s))
            dxc_t = p.cnt["dxc"]
            p.wait("vector", "dxc", dxc_t)
            if ec >= 2:
                p.wait("vector", "st", mm_done[ec - 2])
            else:
                p.wait("vector", "st", prev_mm_tail[ec])
            a = cslot[:].rearrange("q (dc j) -> q dc j", dc=NDC)
            f = fslot[:].rearrange("q (dc j) -> q dc j", dc=NDC)
            p.op("vector", lambda e, a=a, f=f: e.tensor_tensor(
                out=f[:, :, 0:64], in0=a[:, :, 0:64], in1=a[:, :, 64:128], op=ALU.subtract))
            fold_done[ec] = p.op("vector", lambda e, a=a, f=f: e.tensor_tensor(
                out=f[:, :, 64:128], in0=a[:, :, 64:128], in1=a[:, :, 0:64], op=ALU.add))

        emit_load_fold(0)
        for ec in range(n_cols):
            if ec + 1 < n_cols:
                emit_load_fold(ec + 1)
            fslot = wfold[ec % 2]
            for tb in range(n_tb):
                it = ec * n_tb + tb
                bank = psr[it % 2]
                prev = ev[it - 2] if it >= 2 else prev_tail[it]
                p.wait("tensor", "sv", prev)
                if tb == 0:
                    p.wait("tensor", "sv", fold_done[ec])
                for dc in range(NDC):
                    fn = (lambda e, b=bank, fs=fslot, dc=dc, tb=tb:
                          e.matmul(b[:], fs[:, dc * 128:(dc + 1) * 128],
                                   x_fm[:, dc * NSEQ + tb * 512: dc * NSEQ + (tb + 1) * 512],
                                   start=(dc == 0), stop=(dc == NDC - 1)))
                    if dc < NDC - 1:
                        p.traw(fn)
                    else:
                        mm_done[ec] = p.tclose(fn)
                p.wait("vector", "st", mm_done[ec])
                if it == 0:
                    p.wait("vector", "dl", dl_cos)
                cos_sl = cosk_sb[:, tb * 512:(tb + 1) * 512]
                dstq = out_fm[:, ec * out_stride + tb * 512: ec * out_stride + (tb + 1) * 512]
                ev[it] = p.op("vector", lambda e, b=bank, c=cos_sl, d=dstq:
                              e.tensor_tensor(out=d, in0=b[:], in1=c, op=ALU.mult))
        return ev, fold_done, mm_done

    q_ev, q_folds, q_mms = proj_stream(H, wq_d, q_fm, TQ, TQ // 512, [0, 0])
    k_ev, k_folds, k_mms = proj_stream(HKV, wk_d, k_fm, NSEQ, NSEQ // 512, q_ev[-2:],
                                       prev_fold_tail=q_folds[-2:],
                                       prev_mm_tail=q_mms[-2:])

    p.barrier()
    vslots = [pA[:, 0:512], pA[:, 512:1024], pB[:, 0:512], pB[:, 512:1024],
              pC[:], pD[:], pT32[:, 0:512], pT32[:, 512:1024]]
    v_ev = {}
    wv_mm_last = 0
    for tkg in range(2):
        for half in range(2):
            src = wv_d[half * 1024:(half + 1) * 1024, :].rearrange("(dc q) j -> q dc j", q=128)
            dstv = wv_half[:].rearrange("q (dc j) -> q dc j", dc=8)
            if wv_mm_last:
                p.wait("gpsimd", "st", wv_mm_last)
            p.dma("gpsimd", "dxc", lambda e, s=src, d=dstv: e.dma_start(out=d, in_=s))
            dxc_t = p.cnt["dxc"]
            p.wait("tensor", "dxc", dxc_t)
            for tki in range(8):
                tkc = tkg * 8 + tki
                if half == 0 and tkg > 0:
                    p.wait("tensor", "sa", v_ev[tki])
                for dc8 in range(8):
                    dc = half * 8 + dc8
                    fn = (lambda e, b=vslots[tki], dc=dc, dc8=dc8, tkc=tkc:
                          e.matmul(b,
                                   x_fm[:, dc * NSEQ + tkc * 128: dc * NSEQ + tkc * 128 + 128],
                                   wv_half[:, dc8 * 512:(dc8 + 1) * 512],
                                   start=(dc == 0), stop=(dc == 15)))
                    if not (tki == 7 and dc8 == 7):
                        p.traw(fn)
                    else:
                        wv_mm_last = p.tclose(fn)
        p.wait("scalar", "st", wv_mm_last)
        for tki in range(8):
            tkc = tkg * 8 + tki
            v_ev[tki] = p.op("scalar", lambda e, b=vslots[tki], tkc=tkc:
                             e.activation(out=v_tok[:, tkc * 512:(tkc + 1) * 512],
                                          in_=b, func=AF.Copy))

    tap("q_fm", q_fm[:], [128, H * TQ], BF16)
    tap("k_fm", k_fm[:], [128, HKV * NSEQ], BF16)
    tap("v_tok", v_tok[:], [128, NKC * 512], BF16)

    p.barrier()
    pool.release(gX)

    gP = []
    sbP = lambda n, s, d: (gP.append(pool.alloc(n, s, d)) or gP[-1])
    qsq = [sbP(f"qsq{i}", [128, NSEQ], BF16) for i in range(2)]
    brow_bf = sbP("brow_bf", [1, 16], BF16)
    attnout = pool.alloc("attnout", [128, H * TQ], BF16)
    pt = sbP("pt", [128, NKC * 512], BF16)
    rinv = sbP("rinv", [1, 512], F32)
    bc_sb = sbP("bc_sb", [128, 512], F32)

    red_done = 0
    mm_g = 0
    mm_g_ring = [0, 0]
    for ec in range(H + HKV):
        is_q = ec < H
        width = TQ if is_q else NSEQ
        src_fm = q_fm if is_q else k_fm
        off = ec * TQ if is_q else (ec - H) * NSEQ
        qs_slot = qsq[ec % 2]
        if mm_g_ring[ec % 2]:
            p.wait("scalar", "st", mm_g_ring[ec % 2])
        sq_done = p.op("scalar", lambda e, s=src_fm, o=off, w=width, q=qs_slot:
                       e.activation(out=q[:, 0:w], in_=s[:, o:o + w], func=AF.Square))
        p.wait("tensor", "sa", sq_done)
        if red_done:
            p.wait("tensor", "sv", red_done)
        nblk = width // 512
        for blk in range(nblk):
            dstp = (pA if blk < 2 else pB)[0:1, (blk % 2) * 512:(blk % 2) * 512 + 512]
            fn = (lambda e, d=dstp, blk=blk, q=qs_slot: e.matmul(
                d, onescol_b[:], q[:, blk * 512:(blk + 1) * 512], start=True, stop=True))
            if blk < nblk - 1:
                p.traw(fn)
            else:
                mm_g = p.tclose(fn)
        mm_g_ring[ec % 2] = mm_g
        p.wait("vector", "st", mm_g)
        col = ec if is_q else 32 + (ec - H)
        p.op("vector", lambda e, col=col: e.reduce_max(
            out=qkrow[0:1, col:col + 1], in_=pA[0:1, 0:1024], axis=AX.X))
        if not is_q:
            p.op("vector", lambda e: e.reduce_max(
                out=qkrow[0:1, 40:41], in_=pB[0:1, 0:1024], axis=AX.X))
            red_done = p.op("vector", lambda e, col=col: e.tensor_tensor(
                out=qkrow[0:1, col:col + 1], in0=qkrow[0:1, col:col + 1],
                in1=qkrow[0:1, 40:41], op=ALU.max))
        else:
            red_done = p.cnt["sv"]
    prods = 0
    for h in range(H):
        prods = p.op("vector", lambda e, h=h: e.tensor_tensor(
            out=qkrow[0:1, 16 + h:17 + h], in0=qkrow[0:1, h:h + 1],
            in1=qkrow[0:1, 32 + h // 4:33 + h // 4], op=ALU.mult))
    p.wait("scalar", "sv", prods)
    sq2 = p.op("scalar", lambda e: e.activation(
        out=qkrow[0:1, 48:64], in_=qkrow[0:1, 16:32], func=AF.Sqrt))
    p.wait("vector", "sa", sq2)
    p.op("vector", lambda e: e.tensor_scalar_mul(out=qkrow[0:1, 48:64],
                                                 in0=qkrow[0:1, 48:64],
                                                 scalar1=-SCALE * 1.02))
    brow_done = p.op("vector", lambda e: e.tensor_copy(out=brow_bf[:], in_=qkrow[0:1, 48:64]))
    p.wait("tensor", "sv", brow_done)
    p.wait("tensor", "sv", red_done)
    gbias = p.tclose(lambda e: e.matmul(pC[:, 0:16], onesrow_b[:], brow_bf[:],
                                        start=True, stop=True))
    p.wait("scalar", "st", gbias)
    bias_done = p.op("scalar", lambda e: e.activation(out=bias_bc[:], in_=pC[:, 0:16],
                                                      func=AF.Copy))
    tap("bias_bc", bias_bc[:], [128, H], F32)

    p.barrier()
    exp_cnt = {}
    avsub = {}
    av_banks = [pC[:], pT32[:, 512:1024]]
    av_close = {}
    av_ev = {}
    rv_ev = {}
    bc_mm = {}
    bc_ev = {}
    n_it = 0

    def attn_finish(it):
        qs_f = (it // 2) * TQ + (it % 2) * 512
        p.wait("vector", "st", avsub[(it, 7)])
        rv_ev[it] = p.op("vector", lambda e: e.reciprocal(out=rinv[:], in_=pD[0:1, 0:512]))
        p.wait("tensor", "sv", rv_ev[it])
        if it > 0:
            p.wait("tensor", "sa", bc_ev[it - 1])
        bc_mm[it] = p.tclose(lambda e: e.matmul(pT32[:, 0:512], onesrow_f[:], rinv[:],
                                                start=True, stop=True))
        p.wait("scalar", "st", bc_mm[it])
        if it > 0:
            p.wait("scalar", "sv", av_ev[it - 1])
        bc_ev[it] = p.op("scalar", lambda e: e.activation(out=bc_sb[:], in_=pT32[:, 0:512],
                                                          func=AF.Copy))
        p.wait("vector", "sa", bc_ev[it])
        av_ev[it] = p.op("vector", lambda e, qs_f=qs_f, b=av_banks[it % 2]: e.tensor_tensor(
            out=attnout[:, qs_f:qs_f + 512], in0=b, in1=bc_sb[:], op=ALU.mult))

    for h in range(H):
        kv = h // 4
        for qb in range(TQ // 512):
            it = n_it
            n_it += 1
            qs = h * TQ + qb * 512
            st_g = {}
            for g in range(8):
                bank = pA if g % 2 == 0 else pB
                if (it, g - 2) in exp_cnt:
                    p.wait("tensor", "sa", exp_cnt[(it, g - 2)])
                elif it > 0 and g < 2:
                    p.wait("tensor", "sa", exp_cnt[(it - 1, 6 + g)])
                for ci in range(2):
                    c = 2 * g + ci
                    fn = (lambda e, b=bank, kv=kv, c=c, ci=ci, qs=qs:
                          e.matmul(b[:, ci * 512:(ci + 1) * 512],
                                   k_fm[:, kv * NSEQ + c * 128: kv * NSEQ + c * 128 + 128],
                                   q_fm[:, qs:qs + 512], start=True, stop=True))
                    if ci == 0:
                        p.traw(fn)
                    else:
                        st_g[g] = p.tclose(fn)
                p.wait("scalar", "st", st_g[g])
                if it > 0:
                    p.wait("scalar", "st", avsub[(it - 1, g)])
                if it == 0 and g == 0:
                    p.wait("scalar", "sa", bias_done)
                exp_cnt[(it, g)] = p.op("scalar", lambda e, b=bank, g=g, h=h: e.activation(
                    out=pt[:, 2 * g * 512: 2 * g * 512 + 1024], in_=b[:],
                    func=AF.Exp, bias=bias_bc[:, h:h + 1], scale=SCALE))
            if it > 0:
                attn_finish(it - 1)
            avb = av_banks[it % 2]
            for g in range(8):
                p.wait("tensor", "sa", exp_cnt[(it, g)])
                if g == 0:
                    if it >= 2:
                        p.wait("tensor", "sv", av_ev[it - 2])
                    if it > 0:
                        p.wait("tensor", "sv", rv_ev[it - 1])
                for ci in range(2):
                    c = 2 * g + ci
                    p.traw(lambda e, kv=kv, c=c, avb=avb: e.matmul(
                        avb,
                        v_tok[:, c * 512 + kv * 128: c * 512 + kv * 128 + 128],
                        pt[:, c * 512:(c + 1) * 512],
                        start=(c == 0), stop=(c == NKC - 1)))
                    fn = (lambda e, c=c: e.matmul(
                        pD[0:1, 0:512], onescol_b[:], pt[:, c * 512:(c + 1) * 512],
                        start=(c == 0), stop=(c == NKC - 1)))
                    if ci < 1:
                        p.traw(fn)
                    else:
                        avsub[(it, g)] = p.tclose(fn)
    attn_finish(n_it - 1)

    tap("attnout", attnout[:], [128, H * TQ], BF16)
    p.barrier()
    pool.release(gP)
    pool.release(gQKV)

    gWO = []
    sbW = lambda n, s, d: (gWO.append(pool.alloc(n, s, d)) or gWO[-1])
    x1pre = sbW("x1pre", [128, NTC * D], F32)
    wo_col = [sbW(f"wo_col{i}", [128, H * 512], BF16) for i in range(2)]
    xown = [sbW(f"xown{i}", [128, D], F32) for i in range(2)]

    wo_ev = {}
    xown_user = [0, 0]
    wo_mm_close = {}
    for db in range(4):
        src = wo_d[:, db * 512:(db + 1) * 512].rearrange("(ec q) j -> q ec j", q=128)
        dstv = wo_col[db % 2][:].rearrange("q (ec j) -> q ec j", ec=H)
        if db >= 2:
            p.wait("gpsimd", "st", wo_mm_close[db - 2])
        p.dma("gpsimd", "dxc", lambda e, s=src, d=dstv: e.dma_start(out=d, in_=s))
        dxc_t = p.cnt["dxc"]
        p.wait("tensor", "dxc", dxc_t)
        for tc in range(NTC):
            sl = xown[tc % 2]
            p.wait("sync", "sv", xown_user[tc % 2])
            p.dma("sync", "dl", lambda e, sl=sl, tc=tc:
                  e.dma_start(out=sl[:], in_=xb[tc * 128:(tc + 1) * 128, :]))
            dl_t = p.cnt["dl"]
            bank = pC if tc % 2 == 0 else pD
            it = db * NTC + tc
            if it >= 2:
                pit = it - 2
                p.wait("tensor", "sv", wo_ev[(pit // NTC, pit % NTC)])
            for ec in range(H):
                fn = (lambda e, b=bank, ec=ec, tc=tc, w=wo_col[db % 2]:
                      e.matmul(b[:], attnout[:, ec * TQ + tc * 128: ec * TQ + tc * 128 + 128],
                               w[:, ec * 512:(ec + 1) * 512],
                               start=(ec == 0), stop=(ec == H - 1)))
                if ec < H - 1:
                    p.traw(fn)
                else:
                    wo_mm_close[db] = p.tclose(fn)
            p.wait("vector", "st", wo_mm_close[db])
            p.wait("vector", "dl", dl_t)
            dst = x1pre[:, tc * D + db * 512: tc * D + (db + 1) * 512]
            wo_ev[(db, tc)] = p.op("vector", lambda e, b=bank, sl=sl, db=db, d=dst:
                                   e.tensor_tensor(out=d, in0=b[:],
                                                   in1=sl[:, db * 512:(db + 1) * 512],
                                                   op=ALU.add))
            xown_user[tc % 2] = wo_ev[(db, tc)]
            if db == 3:
                for blk in range(4):
                    p.op("vector", lambda e, tc=tc, blk=blk: e.bn_stats(
                        out=bnst[:, 6 * blk:6 * blk + 6],
                        in_=x1pre[:, tc * D + 512 * blk: tc * D + 512 * (blk + 1)]))
                p.op("vector", lambda e, tc=tc: e.bn_aggr(
                    out=bnag8[:, 2 * tc:2 * tc + 2], in_=bnst[:]))

    def layernorm(src_ap, dst_ap):
        for blk in range(4):
            p.op("vector", lambda e, blk=blk: e.bn_stats(
                out=bnst[:, 6 * blk:6 * blk + 6], in_=src_ap[:, 512 * blk:512 * (blk + 1)]))
        agg = p.op("vector", lambda e: e.bn_aggr(out=bnag[:], in_=bnst[:]))
        p.wait("scalar", "sv", agg)
        sqv = p.op("scalar", lambda e: e.activation(out=colp[:, 0:1], in_=bnag[:, 1:2],
                                                    func=AF.Sqrt, bias=epscol[:, 0:1]))
        p.wait("vector", "sa", sqv)
        p.op("vector", lambda e: e.reciprocal(out=colp[:, 1:2], in_=colp[:, 0:1]))
        nmr = p.op("vector", lambda e: e.scalar_tensor_tensor(
            out=colp[:, 2:3], in0=bnag[:, 0:1], scalar=-1.0, in1=colp[:, 1:2],
            op0=ALU.mult, op1=ALU.mult))
        p.wait("scalar", "sv", nmr)
        return p.op("scalar", lambda e: e.activation(out=dst_ap, in_=src_ap,
                                                     func=AF.Identity,
                                                     scale=colp[:, 1:2], bias=colp[:, 2:3]))

    agg8 = p.cnt["sv"]
    p.wait("scalar", "sv", agg8)
    var8 = bnag8[:].rearrange("q (a b) -> q a b", b=2)[:, :, 1:2]
    mean8 = bnag8[:].rearrange("q (a b) -> q a b", b=2)[:, :, 0:1]
    sq8 = p.op("scalar", lambda e: e.activation(
        out=lncol[:, 0:8].rearrange("q (a b) -> q a b", b=1), in_=var8,
        func=AF.Sqrt, bias=epscol[:, 0:1]))
    p.wait("vector", "sa", sq8)
    p.op("vector", lambda e: e.reciprocal(out=lncol[:, 8:16], in_=lncol[:, 0:8]))
    nmr8 = p.op("vector", lambda e: e.scalar_tensor_tensor(
        out=lncol[:, 16:24].rearrange("q (a b) -> q a b", b=1), in0=mean8, scalar=-1.0,
        in1=lncol[:, 8:16].rearrange("q (a b) -> q a b", b=1),
        op0=ALU.mult, op1=ALU.mult))
    p.wait("scalar", "sv", nmr8)
    ln1_done = [0] * NTC
    for tc in range(NTC):
        ln1_done[tc] = p.op("scalar", lambda e, tc=tc: e.activation(
            out=x1[:, tc * D:(tc + 1) * D], in_=x1pre[:, tc * D:(tc + 1) * D],
            func=AF.Identity, scale=lncol[:, 8 + tc:9 + tc], bias=lncol[:, 16 + tc:17 + tc]))

    tap("x1", x1[:], [128, NTC * D], BF16)
    p.barrier()
    pool.release([attnout])
    pool.release(gWO)

    gXQ = []
    sbq = lambda n, s, d: (gXQ.append(pool.alloc(n, s, d)) or gXQ[-1])
    xq_fm = sbq("xq_fm", [128, NDC * TQ], BF16)
    xq_tok = sbq("xq_tok", [128, D], BF16)
    sxrow = sbq("sxrow", [1, TQ], F32)
    sxbc = sbq("sxbc", [128, TQ], F32)

    h_fm = pool.alloc("h_fm", [128, NIC * TQ], BF16)
    gF1 = []
    sbf = lambda n, s, d: (gF1.append(pool.alloc(n, s, d)) or gF1[-1])
    stage = [sbf(f"stage{i}", [128, D], BF16) for i in range(2)]
    w1t = [sbf(f"w1t{i}", [128, D], BF16) for i in range(2)]
    wcols = sbf("wcols", [128, 8], F32)
    g1s = sbf("g1s", [128, 512], F32)
    sils = sbf("sils", [128, 512], F32)
    u1s = sbf("u1s", [128, 512], F32)

    gF2 = []
    sbf2 = lambda n, s, d: (gF2.append(pool.alloc(n, s, d)) or gF2[-1])
    tmax = sbf2("tmax", [128, TQ], BF16)
    tmin = sbf2("tmin", [128, TQ], BF16)

    pt_half_ev = [0, 0]
    w1_q = {}
    tr_close = {}
    w1t_ev = {}
    last_w1t_ev = 0
    last_silu = 0
    ffn1_ev = {}

    def ffn1_quant(osl):
        par = osl % 2
        for gu in range(2):
            slot = stage[par * 2 + gu]
            rows = w1_d[gu * II + osl * 128: gu * II + (osl + 1) * 128, :]
            dep = tr_close.get((osl - 2, gu), 0)
            if dep:
                p.wait("gpsimd", "st", dep)
            p.dma("gpsimd", "dxc", lambda e, s=rows, d=slot: e.dma_start(out=d[:], in_=s))
            dxc_t = p.cnt["dxc"]
            icol = wcols[:, par * 4 + gu: par * 4 + gu + 1]
            swc = wcols[:, par * 4 + 2 + gu: par * 4 + 3 + gu]
            p.wait("vector", "dxc", dxc_t)
            p.op("vector", lambda e, s=slot: e.reduce_max(
                out=colp[:, 8:9], in_=s[:], axis=AX.X, apply_absolute_value=True))
            p.op("vector", lambda e: e.tensor_scalar_max(out=colp[:, 8:9], in0=colp[:, 8:9],
                                                         scalar1=AMAX_MIN))
            p.op("vector", lambda e, ic=icol: e.reciprocal(out=ic, in_=colp[:, 8:9]))
            icd = p.op("vector", lambda e, ic=icol: e.tensor_scalar_mul(out=ic, in0=ic,
                                                                        scalar1=127.0))
            if last_silu:
                p.wait("vector", "sa", last_silu)
            p.op("vector", lambda e, sw=swc: e.tensor_scalar_mul(
                out=sw, in0=colp[:, 8:9], scalar1=1.0 / 127.0))
            p.wait("scalar", "sv", icd)
            td = p.op("scalar", lambda e, s=slot, ic=icol: e.activation(
                out=scr_a[:], in_=s[:], func=AF.Identity, scale=ic, bias=mcol[:, 0:1]))
            p.wait("vector", "sa", td)
            w1_q[(osl, gu)] = p.op("vector", lambda e, s=slot: e.tensor_scalar(
                out=s[:], in0=scr_a[:], scalar1=mcol[:, 0:1], scalar2=None,
                op0=ALU.subtract))

    ffn1_quant(0)

    xq_tr = [0] * NTC
    xq_ev = [0] * NTC
    for tc in range(NTC):
        p.op("vector", lambda e, tc=tc: e.reduce_max(
            out=acol8[:, tc:tc + 1], in_=x1[:, tc * D:(tc + 1) * D],
            axis=AX.X, apply_absolute_value=True))
    p.op("vector", lambda e: e.tensor_scalar_max(out=acol8[:, 0:8], in0=acol8[:, 0:8],
                                                 scalar1=AMAX_MIN))
    p.op("vector", lambda e: e.reciprocal(out=acol8[:, 8:16], in_=acol8[:, 0:8]))
    icd = p.op("vector", lambda e: e.tensor_scalar_mul(out=acol8[:, 8:16],
                                                       in0=acol8[:, 8:16], scalar1=127.0))
    sxc = p.op("vector", lambda e: e.tensor_scalar_mul(out=acol8[:, 16:24],
                                                       in0=acol8[:, 0:8],
                                                       scalar1=1.0 / 127.0))
    p.wait("scalar", "sv", icd)
    for tc in range(NTC):
        x1t = x1[:, tc * D:(tc + 1) * D]
        if tc >= 1:
            p.wait("scalar", "sv", xq_q[tc - 1] if tc >= 1 else 0)
        tqd = p.op("scalar", lambda e, x1t=x1t, tc=tc: e.activation(
            out=scr_a[:], in_=x1t, func=AF.Identity, scale=acol8[:, 8 + tc:9 + tc],
            bias=mcol[:, 0:1]))
        p.wait("vector", "sa", tqd)
        if tc >= 1:
            p.wait("vector", "st", xq_tr[tc - 1])
        qd = p.op("vector", lambda e: e.tensor_scalar(
            out=xq_tok[:], in0=scr_a[:], scalar1=mcol[:, 0:1], scalar2=None,
            op0=ALU.subtract))
        xq_q = xq_q if tc >= 1 else [0] * NTC
        xq_q[tc] = qd
        p.wait("tensor", "sv", qd)
        if tc >= 1:
            p.wait("tensor", "sv", xq_ev[tc - 1])
        for dc in range(NDC):
            p.traw(lambda e, dc=dc: e.transpose(pT[:, dc * 128:(dc + 1) * 128],
                                                xq_tok[:, dc * 128:(dc + 1) * 128], idb[:]))
        xq_tr[tc] = p.tclose(lambda e, tc=tc: e.transpose(
            pA[0:1, tc * 128:(tc + 1) * 128], acol8[:, 16 + tc:17 + tc], idf[:]))
        p.wait("vector", "st", xq_tr[tc])
        src = pT[:].rearrange("q (dc t) -> q dc t", dc=NDC)
        dst = xq_fm[:].rearrange("q (dc t) -> q dc t", dc=NDC)[:, :, tc * 128:(tc + 1) * 128]
        xq_ev[tc] = p.op("vector", lambda e, s=src, d=dst: e.tensor_copy(out=d, in_=s))
    srow_done = p.op("vector", lambda e: e.tensor_copy(out=sxrow[:], in_=pA[0:1, 0:TQ]))
    p.wait("tensor", "sv", srow_done)
    gb1 = p.tclose(lambda e: e.matmul(pB[:, 0:512], onesrow_f[:], sxrow[0:1, 0:512],
                                      start=True, stop=True))
    gb2 = p.tclose(lambda e: e.matmul(pB[:, 512:1024], onesrow_f[:], sxrow[0:1, 512:1024],
                                      start=True, stop=True))
    p.wait("scalar", "st", gb2)
    sxbc_done = p.op("scalar", lambda e: e.activation(out=sxbc[:], in_=pB[:], func=AF.Copy))

    tap("xq_fm", xq_fm[:], [128, NDC * TQ], BF16)
    tap("sxbc", sxbc[:], [128, TQ], F32)


    xqb_sv = p.cnt["sv"]
    xqb_sa = p.cnt["sa"]
    gXQ.remove(xq_tok); gXQ.remove(sxrow)
    pool.release([xq_tok, sxrow])
    stage.extend([pool.alloc(f"stage{i}", [128, D], BF16) for i in (2, 3)])
    gF1.extend(stage[2:4])

    for osl in range(NIC):
        par = osl % 2
        stg = [stage[(osl % 2) * 2], stage[(osl % 2) * 2 + 1]]
        if osl == 0:
            tap("w1q0", stage[0][:], [128, D], BF16)
            tap("w1q0u", stage[1][:], [128, D], BF16)
        if osl + 1 < NIC:
            if osl == 0:
                p.wait("gpsimd", "st", xq_tr[NTC - 1])
                p.wait("gpsimd", "sv", xqb_sv)
            ffn1_quant(osl + 1)
        for gu in range(2):
            p.wait("tensor", "sv", w1_q[(osl, gu)])
            if osl == 0 and gu == 0:
                p.wait("tensor", "sv", xqb_sv)
                p.wait("tensor", "sa", xqb_sa)
            for half in range(2):
                bank = (2 * osl + 2 * gu + half) % 2
                hoff = bank * 1024
                if pt_half_ev[bank]:
                    p.wait("tensor", "sa", pt_half_ev[bank])
                for j in range(8):
                    dc = half * 8 + j
                    fn = (lambda e, s=stg[gu], dc=dc, hoff=hoff, j=j: e.transpose(
                        pT[:, hoff + j * 128: hoff + (j + 1) * 128],
                        s[:, dc * 128:(dc + 1) * 128], idb[:]))
                    if j < 7:
                        p.traw(fn)
                    else:
                        tg = p.tclose(fn)
                p.wait("scalar", "st", tg)
                pt_half_ev[bank] = p.op("scalar", lambda e, d=w1t[gu], half=half, hoff=hoff:
                                        e.activation(out=d[:, half * 1024:(half + 1) * 1024],
                                                     in_=pT[:, hoff:hoff + 1024], func=AF.Copy))
            tr_close[(osl, gu)] = tg
            w1t_ev[(osl, gu)] = pt_half_ev[(2 * osl + 2 * gu + 1) % 2]
            if osl == 0 and gu == 0:
                tap("w1t0", w1t[0][:], [128, D], BF16)
        for tb in range(2):
            p.wait("tensor", "sa", max(pt_half_ev))
            prev = ffn1_ev.get((osl - 1, tb), 0)
            if prev:
                p.wait("tensor", "sv", prev)
            gmm = 0
            for which, psx in ((0, pA), (1, pB)):
                for hc in range(NDC):
                    fn = (lambda e, b=psx, w=w1t[which], hc=hc, tb=tb: e.matmul(
                        b[:, tb * 512:(tb + 1) * 512], w[:, hc * 128:(hc + 1) * 128],
                        xq_fm[:, hc * TQ + tb * 512: hc * TQ + (tb + 1) * 512],
                        start=(hc == 0), stop=(hc == NDC - 1)))
                    if not (which == 1 and hc == NDC - 1):
                        p.traw(fn)
                    else:
                        gmm = p.tclose(fn)
            p.wait("vector", "st", gmm)
            if tb == 0 and osl == 0:
                p.wait("vector", "sa", sxbc_done)
            gg = p.op("vector", lambda e, tb=tb: e.tensor_tensor(
                out=g1s[:], in0=pA[:, tb * 512:(tb + 1) * 512],
                in1=sxbc[:, tb * 512:(tb + 1) * 512], op=ALU.mult))
            p.wait("scalar", "sv", gg)
            sil_done = p.op("scalar", lambda e, sw=wcols[:, par * 4 + 2:par * 4 + 3]:
                            e.activation(out=sils[:], in_=g1s[:], func=AF.Silu, scale=sw))
            last_silu = sil_done
            p.op("vector", lambda e, tb=tb: e.tensor_tensor(
                out=u1s[:], in0=pB[:, tb * 512:(tb + 1) * 512],
                in1=sxbc[:, tb * 512:(tb + 1) * 512], op=ALU.mult))
            if osl == 0 and tb == 0:
                tap("g1s0", g1s[:], [128, 512], F32)
                tap("u1s0", u1s[:], [128, 512], F32)
            p.wait("vector", "sa", sil_done)
            ffn1_ev[(osl, tb)] = p.op(
                "vector", lambda e, osl=osl, tb=tb, sw=wcols[:, par * 4 + 3:par * 4 + 4]:
                e.scalar_tensor_tensor(
                    out=h_fm[:, osl * TQ + tb * 512: osl * TQ + (tb + 1) * 512],
                    in0=u1s[:], scalar=sw, in1=sils[:], op0=ALU.mult, op1=ALU.mult))
        if osl == 0:
            p.op("vector", lambda e: e.tensor_copy(out=tmax[:], in_=h_fm[:, 0:TQ]))
            p.op("vector", lambda e: e.tensor_copy(out=tmin[:], in_=h_fm[:, 0:TQ]))
        else:
            p.op("vector", lambda e, osl=osl: e.tensor_tensor(
                out=tmax[:], in0=tmax[:], in1=h_fm[:, osl * TQ:(osl + 1) * TQ], op=ALU.max))
            p.op("vector", lambda e, osl=osl: e.tensor_tensor(
                out=tmin[:], in0=tmin[:], in1=h_fm[:, osl * TQ:(osl + 1) * TQ], op=ALU.min))

    tap("h_fm", h_fm[:], [128, NIC * TQ], BF16)
    p.barrier()
    pool.release(gXQ)
    pool.release(gF1)

    hrow = sbf2("hrow", [1, TQ], F32)
    irow = sbf2("irow", [1, TQ], F32)
    i127bc = sbf2("i127bc", [128, TQ], BF16)
    shbc = sbf2("shbc", [128, TQ], F32)
    tm_done = p.cnt["sv"]
    p.wait("tensor", "sv", tm_done)
    for tc in range(NTC):
        p.traw(lambda e, tc=tc: e.transpose(
            pT[:, tc * 128:(tc + 1) * 128], tmax[:, tc * 128:(tc + 1) * 128], idb[:]))
        fn = (lambda e, tc=tc: e.transpose(
            pT[:, 1024 + tc * 128: 1024 + (tc + 1) * 128],
            tmin[:, tc * 128:(tc + 1) * 128], idb[:]))
        if tc < NTC - 1:
            p.traw(fn)
        else:
            t3g = p.tclose(fn)
    p.wait("vector", "st", t3g)
    micro = 0
    for tc in range(NTC):
        if micro:
            p.wait("vector", "st", micro)
        p.op("vector", lambda e, tc=tc: e.reduce_max(
            out=colp[:, 5:6], in_=pT[:, tc * 128:(tc + 1) * 128], axis=AX.X,
            apply_absolute_value=True))
        p.op("vector", lambda e, tc=tc: e.reduce_max(
            out=colp[:, 6:7], in_=pT[:, 1024 + tc * 128: 1024 + (tc + 1) * 128], axis=AX.X,
            apply_absolute_value=True))
        p.op("vector", lambda e: e.tensor_tensor(out=colp[:, 5:6], in0=colp[:, 5:6],
                                                 in1=colp[:, 6:7], op=ALU.max))
        p.op("vector", lambda e: e.tensor_scalar_max(out=colp[:, 5:6], in0=colp[:, 5:6],
                                                     scalar1=AMAX_MIN))
        amc = p.op("vector", lambda e: e.tensor_copy(out=amaxf[:], in_=colp[:, 5:6]))
        p.wait("tensor", "sv", amc)
        micro = p.tclose(lambda e, tc=tc: e.transpose(
            pA[0:1, tc * 128:(tc + 1) * 128], amaxf[:], idf[:]))
    p.wait("vector", "st", micro)
    p.op("vector", lambda e: e.tensor_copy(out=hrow[:], in_=pA[0:1, 0:TQ]))
    p.op("vector", lambda e: e.reciprocal(out=irow[:], in_=hrow[:]))
    ird = p.op("vector", lambda e: e.tensor_scalar_mul(out=irow[:], in0=irow[:],
                                                       scalar1=127.0))
    shd = p.op("vector", lambda e: e.tensor_scalar_mul(out=hrow[:], in0=hrow[:],
                                                       scalar1=1.0 / 127.0))
    p.wait("tensor", "sv", ird)
    gb1 = p.tclose(lambda e: e.matmul(pB[:, 0:512], onesrow_f[:], irow[0:1, 0:512],
                                      start=True, stop=True))
    gb2 = p.tclose(lambda e: e.matmul(pB[:, 512:1024], onesrow_f[:], irow[0:1, 512:1024],
                                      start=True, stop=True))
    p.wait("scalar", "st", gb2)
    ibc_done = p.op("scalar", lambda e: e.activation(out=i127bc[:], in_=pB[:], func=AF.Copy))
    p.wait("tensor", "sa", ibc_done)
    p.wait("tensor", "sv", shd)
    gb3 = p.tclose(lambda e: e.matmul(pB[:, 0:512], onesrow_f[:], hrow[0:1, 0:512],
                                      start=True, stop=True))
    gb4 = p.tclose(lambda e: e.matmul(pB[:, 512:1024], onesrow_f[:], hrow[0:1, 512:1024],
                                      start=True, stop=True))
    p.wait("scalar", "st", gb4)
    shbc_done = p.op("scalar", lambda e: e.activation(out=shbc[:], in_=pB[:], func=AF.Copy))

    gF2.remove(tmax); gF2.remove(tmin); gF2.remove(hrow); gF2.remove(irow)
    pool.release([tmax, tmin, hrow, irow])
    p.barrier()
    gF3 = []
    sbf3 = lambda n, s, d: (gF3.append(pool.alloc(n, s, d)) or gF3[-1])
    outfm = pool.alloc("outfm", [128, NDC * TQ], BF16)
    stage2 = [sbf3("stage2_0", [128, II], BF16)]
    w2t = sbf3("w2t", [128, II], BF16)
    wcols2 = sbf3("wcols2", [128, 8], F32)
    o1s = sbf3("o1s", [128, 512], F32)
    pt2_half_ev = list(pt_half_ev)
    w2_q = {}
    w2_tr_last = {}
    ffn2_ev = {}

    def ffn2_quant(oc):
        par = oc % 2
        slot2 = stage2[0]
        dep = w2_tr_last.get(oc - 1, 0)
        if dep:
            p.wait("gpsimd", "st", dep)
        p.dma("gpsimd", "dxc", lambda e, oc=oc, s=slot2: e.dma_start(
            out=s[:], in_=w2_d[oc * 128:(oc + 1) * 128, :]))
        dxc_t = p.cnt["dxc"]
        icol = wcols2[:, par * 2: par * 2 + 1]
        swc = wcols2[:, 4 + par: 5 + par]
        p.wait("vector", "dxc", dxc_t)
        p.op("vector", lambda e, s=slot2: e.reduce_max(out=colp[:, 8:9], in_=s[:], axis=AX.X,
                                                       apply_absolute_value=True))
        p.op("vector", lambda e: e.tensor_scalar_max(out=colp[:, 8:9], in0=colp[:, 8:9],
                                                     scalar1=AMAX_MIN))
        p.op("vector", lambda e, ic=icol: e.reciprocal(out=ic, in_=colp[:, 8:9]))
        icd = p.op("vector", lambda e, ic=icol: e.tensor_scalar_mul(out=ic, in0=ic,
                                                                    scalar1=127.0))
        p.op("vector", lambda e, sw=swc: e.tensor_scalar_mul(out=sw, in0=colp[:, 8:9],
                                                             scalar1=1.0 / 127.0))
        prev_q = icd
        for quarter in range(4):
            w0 = quarter * (II // 4)
            p.wait("scalar", "sv", prev_q)
            td = p.op("scalar", lambda e, w0=w0, ic=icol, s=slot2: e.activation(
                out=scr_a[:, 0:II // 4], in_=s[:, w0:w0 + II // 4],
                func=AF.Identity, scale=ic, bias=mcol[:, 0:1]))
            p.wait("vector", "sa", td)
            w2_q[oc] = prev_q = p.op("vector", lambda e, w0=w0, s=slot2: e.tensor_scalar(
                out=s[:, w0:w0 + II // 4], in0=scr_a[:, 0:II // 4],
                scalar1=mcol[:, 0:1], scalar2=None, op0=ALU.subtract))

    last_w2t_ev = 0
    ffn2_quant(0)

    hq_done = [0] * NIC
    p.wait("vector", "sa", ibc_done)
    for ic in range(NIC):
        p.op("vector", lambda e, ic=ic: e.tensor_tensor(
            out=scr_a[:, 0:TQ], in0=h_fm[:, ic * TQ:(ic + 1) * TQ], in1=i127bc[:],
            op=ALU.mult))
        hq_done[ic] = p.op("vector", lambda e, ic=ic: e.tensor_scalar(
            out=h_fm[:, ic * TQ:(ic + 1) * TQ], in0=scr_a[:, 0:TQ],
            scalar1=mcol[:, 0:1], scalar2=mcol[:, 0:1], op0=ALU.add, op1=ALU.subtract))

    tap("hq_fm", h_fm[:], [128, NIC * TQ], BF16)
    tap("shbc", shbc[:], [128, TQ], F32)

    hqb_sv = p.cnt["sv"]

    for oc in range(NDC):
        par = oc % 2
        p.wait("tensor", "sv", w2_q[oc])
        wt_ev = 0
        for batch in range(6):
            b0 = batch * 8
            bn = 8 if batch < 5 else 4
            bank = batch % 2
            hoff = bank * 1024
            if pt2_half_ev[bank]:
                p.wait("tensor", "sa", pt2_half_ev[bank])
            for j in range(bn):
                icx = b0 + j
                fn = (lambda e, icx=icx, j=j, hoff=hoff, s=stage2[0]: e.transpose(
                    pT[:, hoff + j * 128: hoff + (j + 1) * 128],
                    s[:, icx * 128:(icx + 1) * 128], idb[:]))
                if j < bn - 1:
                    p.traw(fn)
                else:
                    tg = p.tclose(fn)
            p.wait("scalar", "st", tg)
            wt_ev = p.op("scalar", lambda e, b0=b0, bn=bn, hoff=hoff: e.activation(
                out=w2t[:, b0 * 128:(b0 + bn) * 128], in_=pT[:, hoff:hoff + bn * 128],
                func=AF.Copy))
            pt2_half_ev[bank] = wt_ev
            w2_tr_last[oc] = tg
        if oc == 0:
            tap("w2t0", w2t[:], [128, II], BF16)
        if oc + 1 < NDC:
            ffn2_quant(oc + 1)
        for tb in range(2):
            p.wait("tensor", "sa", wt_ev)
            if oc == 0 and tb == 1:
                p.wait("tensor", "sv", hqb_sv)
            prev = ffn2_ev.get((oc - 1, tb), 0)
            if prev:
                p.wait("tensor", "sv", prev)
            for icx in range(NIC):
                if oc == 0 and tb == 0 and icx % 11 == 0:
                    p.wait("tensor", "sv", hq_done[min(icx + 10, NIC - 1)])
                fn = (lambda e, icx=icx, tb=tb: e.matmul(
                    pA[:, tb * 512:(tb + 1) * 512], w2t[:, icx * 128:(icx + 1) * 128],
                    h_fm[:, icx * TQ + tb * 512: icx * TQ + (tb + 1) * 512],
                    start=(icx == 0), stop=(icx == NIC - 1)))
                if icx < NIC - 1:
                    p.traw(fn)
                else:
                    gmm = p.tclose(fn)
            p.wait("vector", "st", gmm)
            if oc == 0 and tb == 0:
                p.wait("vector", "sa", shbc_done)
            p.op("vector", lambda e, tb=tb: e.tensor_tensor(
                out=o1s[:], in0=pA[:, tb * 512:(tb + 1) * 512],
                in1=shbc[:, tb * 512:(tb + 1) * 512], op=ALU.mult))
            ffn2_ev[(oc, tb)] = p.op(
                "vector", lambda e, oc=oc, tb=tb, sw=wcols2[:, 4 + par:5 + par]:
                e.tensor_scalar(
                    out=outfm[:, oc * TQ + tb * 512: oc * TQ + (tb + 1) * 512],
                    in0=o1s[:], scalar1=sw, scalar2=None, op0=ALU.mult))

    tap("outfm", outfm[:], [128, NDC * TQ], BF16)
    p.barrier()
    pool.release([h_fm])
    pool.release(gF2)
    pool.release(gF3)

    yln8 = pool.alloc("yln8", [128, NTC * D], F32)
    fin_tr = [0] * NTC
    fin_y = [0] * NTC
    st_out = [0] * NTC
    for tc in range(NTC):
        if tc >= 1:
            p.wait("tensor", "sv", fin_y[tc - 1])
        for oc in range(NDC):
            fn = (lambda e, oc=oc, tc=tc: e.transpose(
                pT[:, oc * 128:(oc + 1) * 128],
                outfm[:, oc * TQ + tc * 128: oc * TQ + tc * 128 + 128], idb[:]))
            if oc < NDC - 1:
                p.traw(fn)
            else:
                fin_tr[tc] = p.tclose(fn)
        p.wait("vector", "st", fin_tr[tc])
        fin_y[tc] = p.op("vector", lambda e, tc=tc: e.scalar_tensor_tensor(
            out=yln8[:, tc * D:(tc + 1) * D], in0=pT[:], scalar=1.0,
            in1=x1[:, tc * D:(tc + 1) * D], op0=ALU.mult, op1=ALU.add))
        for blk in range(4):
            p.op("vector", lambda e, tc=tc, blk=blk: e.bn_stats(
                out=bnst[:, 6 * blk:6 * blk + 6],
                in_=yln8[:, tc * D + 512 * blk: tc * D + 512 * (blk + 1)]))
        p.op("vector", lambda e, tc=tc: e.bn_aggr(
            out=bnag8[:, 2 * tc:2 * tc + 2], in_=bnst[:]))
    agg8b = p.cnt["sv"]
    p.wait("scalar", "sv", agg8b)
    var8b = bnag8[:].rearrange("q (a b) -> q a b", b=2)[:, :, 1:2]
    mean8b = bnag8[:].rearrange("q (a b) -> q a b", b=2)[:, :, 0:1]
    sq8b = p.op("scalar", lambda e: e.activation(
        out=lncol[:, 0:8].rearrange("q (a b) -> q a b", b=1), in_=var8b,
        func=AF.Sqrt, bias=epscol[:, 0:1]))
    p.wait("vector", "sa", sq8b)
    p.op("vector", lambda e: e.reciprocal(out=lncol[:, 8:16], in_=lncol[:, 0:8]))
    nmr8b = p.op("vector", lambda e: e.scalar_tensor_tensor(
        out=lncol[:, 16:24].rearrange("q (a b) -> q a b", b=1), in0=mean8b, scalar=-1.0,
        in1=lncol[:, 8:16].rearrange("q (a b) -> q a b", b=1),
        op0=ALU.mult, op1=ALU.mult))
    p.wait("scalar", "sv", nmr8b)
    for tc in range(NTC):
        ln2 = p.op("scalar", lambda e, tc=tc: e.activation(
            out=yln8[:, tc * D:(tc + 1) * D], in_=yln8[:, tc * D:(tc + 1) * D],
            func=AF.Identity, scale=lncol[:, 8 + tc:9 + tc], bias=lncol[:, 16 + tc:17 + tc]))
        p.wait("sync", "sa", ln2)
        st_out[tc] = p.dma("sync", "dso", lambda e, tc=tc: e.dma_start(
            out=out_d[tc * 128:(tc + 1) * 128, :], in_=yln8[:, tc * D:(tc + 1) * D]))
    p.wait("sync", "dso", p.cnt["dso"])

    sems = {}
    semctx = ExitStack()
    for s in ALL_SEMS:
        sems[s] = semctx.enter_context(nc.semaphore(s))

    def run_stream(eng, items):
        for it in items:
            if it[0] == "w":
                eng.wait_ge(sems[it[1]], it[2])
            elif it[0] == "i":
                inst = it[1](eng)
                inst.then_inc(sems[it[2]], it[3])
            else:
                it[1](eng)

    pool.finalize(ctx)
    with ctx, semctx, nc.Block() as block:
        block.sync(lambda e: run_stream(e, p.stream["sync"]))
        block.scalar(lambda e: run_stream(e, p.stream["scalar"]))
        block.vector(lambda e: run_stream(e, p.stream["vector"]))
        block.tensor(lambda e: run_stream(e, p.stream["tensor"]))
        block.gpsimd(lambda e: run_stream(e, p.stream["gpsimd"]))
    return nc


def _cos_table():
    j = np.arange(0, HD, 2, dtype=np.float64)
    inv_freq = 1.0 / (10000.0 ** (j / HD))
    t = np.arange(NSEQ, dtype=np.float64)
    freqs = t[:, None] * inv_freq[None, :]
    emb = np.concatenate([freqs, freqs], axis=-1)
    return np.cos(emb).T.astype(np.float32)


def _consts():
    return {
        "idb": np.eye(128, dtype=ml_dtypes.bfloat16),
        "idf": np.eye(128, dtype=np.float32),
        "onescol_b": np.ones((128, 1), dtype=ml_dtypes.bfloat16),
        "onesrow_f": np.ones((1, 128), dtype=np.float32),
        "onesrow_b": np.ones((1, 128), dtype=ml_dtypes.bfloat16),
    }


def make_in_maps(x, wq, wk, wv, wo, w1, w2):
    cos = _cos_table()
    consts = _consts()
    in_maps = []
    for core in range(8):
        b, half = core // 2, core % 2
        own = slice(half * TQ, (half + 1) * TQ)
        oth = slice((1 - half) * TQ, (2 - half) * TQ)
        xbp = np.ascontiguousarray(np.concatenate([x[b][own], x[b][oth]], axis=0))
        coskp = np.ascontiguousarray(np.concatenate([cos[:, own], cos[:, oth]], axis=1))
        m = {"xb": xbp, "cosk": coskp, "wq": wq, "wk": wk, "wv": wv, "wo": wo,
             "w1": w1, "w2": w2}
        m.update(consts)
        in_maps.append(m)
    return in_maps


def kernel(L0_attn_x, L0_attn_Wq, L0_attn_Wk, L0_attn_Wv, L0_attn_Wo,
           L0_ffn_w_gate_up, L0_ffn_w_down):
    x = np.ascontiguousarray(np.asarray(L0_attn_x, dtype=np.float32))
    wq = np.ascontiguousarray(np.asarray(L0_attn_Wq, dtype=np.float32))
    wk = np.ascontiguousarray(np.asarray(L0_attn_Wk, dtype=np.float32))
    wv = np.ascontiguousarray(np.asarray(L0_attn_Wv, dtype=np.float32))
    wo = np.ascontiguousarray(np.asarray(L0_attn_Wo, dtype=np.float32))
    w1 = np.ascontiguousarray(np.asarray(L0_ffn_w_gate_up, dtype=np.float32))
    w2 = np.ascontiguousarray(np.asarray(L0_ffn_w_down, dtype=np.float32))

    nc = bass.Bass("TRN2", target_bir_lowering=False)
    build_program(nc)
    in_maps = make_in_maps(x, wq, wk, wv, wo, w1, w2)
    res = run_bass_kernel_spmd(nc, in_maps, core_ids=list(range(8)))

    out = np.empty((4, NSEQ, D), dtype=np.float32)
    for core in range(8):
        b, half = core // 2, core % 2
        out[b, half * TQ:(half + 1) * TQ] = res.results[core]["out"]
    return out


# version 11
# speedup vs baseline: 1.1083x; 1.1083x over previous
import numpy as np
import ml_dtypes
from contextlib import ExitStack

import concourse.bass as bass
import concourse.mybir as mybir
from concourse.bass_utils import run_bass_kernel_spmd

F32 = mybir.dt.float32
BF16 = mybir.dt.bfloat16
AF = mybir.ActivationFunctionType
ALU = mybir.AluOpType
AX = mybir.AxisListType

D = 2048
NSEQ = 2048
H = 16
HKV = 4
HD = 128
II = 5632
TQ = 1024
SCALE = float(HD) ** -0.5
C52 = 1.5 * 2.0 ** 52
C32 = 1.5 * 2.0 ** 23
EPS_LN = 1e-5
AMAX_MIN = 127.0e-8
NDC = D // 128
NIC = II // 128
NTC = TQ // 128
NKC = NSEQ // 128

ALL_SEMS = ("sv", "sa", "sp", "st", "dxc", "dl", "dso")


class Prog:
    ENGS = ("sync", "scalar", "vector", "tensor", "gpsimd")
    SELF = {"vector": "sv", "scalar": "sa", "gpsimd": "sp", "tensor": "st"}

    def __init__(self):
        self.stream = {e: [] for e in self.ENGS}
        self.cnt = {s: 0 for s in ALL_SEMS}

    def wait(self, eng, sem, val):
        if val > 0:
            self.stream[eng].append(("w", sem, val))

    def op(self, eng, fn):
        s = self.SELF[eng]
        self.stream[eng].append(("w", s, self.cnt[s]))
        self.cnt[s] += 1
        self.stream[eng].append(("i", fn, s, 1))
        return self.cnt[s]

    def traw(self, fn):
        self.stream["tensor"].append(("n", fn))

    def tclose(self, fn):
        self.cnt["st"] += 1
        self.stream["tensor"].append(("i", fn, "st", 1))
        return self.cnt["st"]

    def dma(self, eng, sem, fn):
        self.cnt[sem] += 16
        self.stream[eng].append(("i", fn, sem, 16))
        return self.cnt[sem]

    def barrier(self):
        snap = dict(self.cnt)
        for e in self.ENGS:
            own = self.SELF.get(e)
            for s in ALL_SEMS:
                if s != own:
                    self.wait(e, s, snap[s])




class SbufPool:

    def __init__(self, nc):
        self.nc = nc
        self.free = [(0, 1 << 30)]
        self.hwm = 0
        self.items = []
        self.where = {}

    def alloc(self, name, shape, dtype):
        pp = int(np.prod(shape[1:])) * mybir.dt.size(dtype)
        pp = (pp + 31) // 32 * 32
        cands = [(l, s, i) for i, (s, l) in enumerate(self.free) if l >= pp]
        if not cands:
            raise MemoryError(f"SbufPool: no slot for {name} ({pp}B)")
        l, s, i = min(cands)
        addr = s
        if l == pp:
            self.free.pop(i)
        else:
            self.free[i] = (s + pp, l - pp)
        h = self.nc.virtual_sbuf_tensor("v_" + name, shape, dtype)
        self.items.append((h, addr))
        self.where[h.name] = (addr, pp)
        self.hwm = max(self.hwm, addr + pp)
        return h

    def release(self, handles):
        for h in handles:
            addr, size = self.where.pop(h.name)
            self.free.append((addr, size))
        self.free.sort()
        merged = []
        for s, l in self.free:
            if merged and merged[-1][0] + merged[-1][1] == s:
                merged[-1][1] += l
            else:
                merged.append([s, l])
        self.free = [(s, l) for s, l in merged]

    def finalize(self, ctx):
        arena = ctx.enter_context(
            self.nc.sbuf_tensor("sbuf_pool_arena", [128, self.hwm], mybir.dt.uint8))
        base = self.nc.lookup_mloc(arena).addr
        for h, addr in self.items:
            self.nc._tensor(h.name, h.shape, h.dtype, type="SB")
            mloc = self.nc.lookup_mloc(h)
            mloc.allocated = not self.nc.target_bir_lowering
            mloc.addr = base + addr


def build_program(nc, dbg=None):
    dbg = list(dbg or ())
    p = Prog()
    ctx = ExitStack()

    xb = nc.dram_tensor("xb", [NSEQ, D], F32, kind="ExternalInput").ap()
    cosk_d = nc.dram_tensor("cosk", [128, NSEQ], F32, kind="ExternalInput").ap()
    wq_d = nc.dram_tensor("wq", [D, H * HD], F32, kind="ExternalInput").ap()
    wk_d = nc.dram_tensor("wk", [D, HKV * HD], F32, kind="ExternalInput").ap()
    wv_d = nc.dram_tensor("wv", [D, HKV * HD], F32, kind="ExternalInput").ap()
    wo_d = nc.dram_tensor("wo", [H * HD, D], F32, kind="ExternalInput").ap()
    w1_d = nc.dram_tensor("w1", [2 * II, D], F32, kind="ExternalInput").ap()
    w2_d = nc.dram_tensor("w2", [D, II], F32, kind="ExternalInput").ap()
    idb_d = nc.dram_tensor("idb", [128, 128], BF16, kind="ExternalInput").ap()
    idf_d = nc.dram_tensor("idf", [128, 128], F32, kind="ExternalInput").ap()
    onescol_d = nc.dram_tensor("onescol_b", [128, 1], BF16, kind="ExternalInput").ap()
    onesrow_f_d = nc.dram_tensor("onesrow_f", [1, 128], F32, kind="ExternalInput").ap()
    onesrow_b_d = nc.dram_tensor("onesrow_b", [1, 128], BF16, kind="ExternalInput").ap()
    out_d = nc.dram_tensor("out", [TQ, D], F32, kind="ExternalOutput").ap()

    def tap(name, ap, shape, dtype):
        if name not in dbg:
            return
        t = nc.dram_tensor("dbg_" + name, shape, dtype, kind="ExternalOutput").ap()
        p.barrier()
        p.dma("sync", "dso", lambda e, a=ap, t=t: e.dma_start(out=t, in_=a))

    pool = SbufPool(nc)
    sb = lambda n, s, d: pool.alloc(n, s, d)

    pA = ctx.enter_context(nc.psum_tensor("pA", [128, 1024], F32))
    pB = ctx.enter_context(nc.psum_tensor("pB", [128, 1024], F32))
    pT = ctx.enter_context(nc.psum_tensor("pT", [128, 2048], BF16))
    pC = ctx.enter_context(nc.psum_tensor("pC", [128, 512], F32))
    pD = ctx.enter_context(nc.psum_tensor("pD", [128, 512], F32))
    pT32 = pT[:].bitcast(F32)

    idb = sb("idb", [128, 128], BF16)
    idf = sb("idf", [128, 128], F32)
    onescol_b = sb("onescol_b", [128, 1], BF16)
    onesrow_f = sb("onesrow_f", [1, 128], F32)
    onesrow_b = sb("onesrow_b", [1, 128], BF16)
    mcol = sb("mcol", [128, 1], F32)
    negm = sb("negm", [128, 1], F32)
    epscol = sb("epscol", [128, 1], F32)
    x1 = sb("x1", [128, NTC * D], BF16)
    bias_bc = sb("bias_bc", [128, H], F32)
    colp = sb("colp", [128, 16], F32)
    qkrow = sb("qkrow", [1, 64], F32)
    scr_a = sb("scr_a", [128, D], F32)
    bnst = sb("bnst", [128, 24], F32)
    bnag = sb("bnag", [128, 2], F32)
    bnag8 = sb("bnag8", [128, 16], F32)
    lncol = sb("lncol", [128, 24], F32)
    acol8 = sb("acol8", [128, 24], F32)
    amaxf = sb("amaxf", [128, 1], F32)

    for dst, src in ((idb, idb_d), (idf, idf_d), (onescol_b, onescol_d),
                     (onesrow_f, onesrow_f_d), (onesrow_b, onesrow_b_d)):
        p.dma("sync", "dl", lambda e, d=dst, s=src: e.dma_start(out=d[:], in_=s))
    dl_consts = p.cnt["dl"]

    p.op("vector", lambda e: e.memset(mcol[:], 1.0))
    p.op("vector", lambda e: e.tensor_scalar(out=mcol[:], in0=mcol[:], scalar1=C52,
                                             scalar2=-C52, op0=ALU.add, op1=ALU.add))
    p.op("vector", lambda e: e.tensor_scalar(out=mcol[:], in0=mcol[:], scalar1=(C52 - C32),
                                             scalar2=C32, op0=ALU.mult, op1=ALU.add))
    p.op("vector", lambda e: e.memset(epscol[:], EPS_LN))
    p.op("vector", lambda e: e.tensor_scalar_mul(out=negm[:], in0=mcol[:], scalar1=-1.0))

    gX = []
    sbX = lambda n, s, d: (gX.append(pool.alloc(n, s, d)) or gX[-1])
    cosk_sb = sbX("cosk_sb", [128, NSEQ], F32)
    x_fm = sbX("x_fm", [128, NDC * NSEQ], BF16)
    ring = [sbX(f"ring{i}", [128, D], BF16) for i in range(2)]
    wfold = [sbX(f"wfold{i}", [128, D], BF16) for i in range(2)]
    wv_half = sbX("wv_half", [128, 8 * 512], BF16)

    p.dma("sync", "dl", lambda e: e.dma_start(out=cosk_sb[:], in_=cosk_d))
    dl_cos = p.cnt["dl"]

    tr_done = [0] * NKC
    evac_done = [0] * NKC
    for tc in range(NKC):
        slot = ring[tc % 2]
        if tc >= 2:
            p.wait("gpsimd", "st", tr_done[tc - 2])
        p.dma("gpsimd", "dxc",
              lambda e, s=slot, tc=tc: e.dma_start(out=s[:], in_=xb[tc * 128:(tc + 1) * 128, :]))
        dxc_t = p.cnt["dxc"]
        p.wait("tensor", "dxc", dxc_t)
        if tc == 0:
            p.wait("tensor", "dl", dl_consts)
        if tc >= 1:
            p.wait("tensor", "sv", evac_done[tc - 1])
        for dc in range(NDC):
            fn = (lambda e, s=slot, dc=dc:
                  e.transpose(pT[:, dc * 128:(dc + 1) * 128],
                              s[:, dc * 128:(dc + 1) * 128], idb[:]))
            if dc < NDC - 1:
                p.traw(fn)
            else:
                tr_done[tc] = p.tclose(fn)
        p.wait("vector", "st", tr_done[tc])
        src = pT[:].rearrange("q (dc t) -> q dc t", dc=NDC)
        dst = x_fm[:].rearrange("q (dc t) -> q dc t", dc=NDC)[:, :, tc * 128:(tc + 1) * 128]
        evac_done[tc] = p.op("vector", lambda e, s=src, d=dst: e.tensor_copy(out=d, in_=s))

    tap("x_fm", x_fm[:], [128, NDC * NSEQ], BF16)

    gQKV = []
    sbQ = lambda n, s, d: (gQKV.append(pool.alloc(n, s, d)) or gQKV[-1])
    q_fm = sbQ("q_fm", [128, H * TQ], BF16)
    k_fm = sbQ("k_fm", [128, HKV * NSEQ], BF16)
    v_tok = sbQ("v_tok", [128, NKC * 512], BF16)

    p.barrier()

    def proj_stream(n_cols, w_dram, out_fm, out_stride, n_tb, prev_tail,
                    prev_fold_tail=(0, 0), prev_mm_tail=(0, 0)):
        psr = [pC, pD]
        fold_done = [0] * (n_cols + 1)
        mm_done = [0] * n_cols
        ev = [0] * (n_cols * n_tb)

        def emit_load_fold(ec):
            cslot, fslot = ring[ec % 2], wfold[ec % 2]
            if ec >= 2:
                p.wait("gpsimd", "sv", fold_done[ec - 2])
            else:
                p.wait("gpsimd", "sv", prev_fold_tail[ec])
            src = w_dram[:, ec * 128:(ec + 1) * 128].rearrange("(dc q) j -> q dc j", q=128)
            dstv = cslot[:].rearrange("q (dc j) -> q dc j", dc=NDC)
            p.dma("gpsimd", "dxc", lambda e, s=src, d=dstv: e.dma_start(out=d, in_=s))
            dxc_t = p.cnt["dxc"]
            p.wait("vector", "dxc", dxc_t)
            if ec >= 2:
                p.wait("vector", "st", mm_done[ec - 2])
            else:
                p.wait("vector", "st", prev_mm_tail[ec])
            a = cslot[:].rearrange("q (dc j) -> q dc j", dc=NDC)
            f = fslot[:].rearrange("q (dc j) -> q dc j", dc=NDC)
            p.op("vector", lambda e, a=a, f=f: e.tensor_tensor(
                out=f[:, :, 0:64], in0=a[:, :, 0:64], in1=a[:, :, 64:128], op=ALU.subtract))
            fold_done[ec] = p.op("vector", lambda e, a=a, f=f: e.tensor_tensor(
                out=f[:, :, 64:128], in0=a[:, :, 64:128], in1=a[:, :, 0:64], op=ALU.add))

        emit_load_fold(0)
        for ec in range(n_cols):
            if ec + 1 < n_cols:
                emit_load_fold(ec + 1)
            fslot = wfold[ec % 2]
            for tb in range(n_tb):
                it = ec * n_tb + tb
                bank = psr[it % 2]
                prev = ev[it - 2] if it >= 2 else prev_tail[it]
                p.wait("tensor", "sv", prev)
                if tb == 0:
                    p.wait("tensor", "sv", fold_done[ec])
                for dc in range(NDC):
                    fn = (lambda e, b=bank, fs=fslot, dc=dc, tb=tb:
                          e.matmul(b[:], fs[:, dc * 128:(dc + 1) * 128],
                                   x_fm[:, dc * NSEQ + tb * 512: dc * NSEQ + (tb + 1) * 512],
                                   start=(dc == 0), stop=(dc == NDC - 1)))
                    if dc < NDC - 1:
                        p.traw(fn)
                    else:
                        mm_done[ec] = p.tclose(fn)
                p.wait("vector", "st", mm_done[ec])
                if it == 0:
                    p.wait("vector", "dl", dl_cos)
                cos_sl = cosk_sb[:, tb * 512:(tb + 1) * 512]
                dstq = out_fm[:, ec * out_stride + tb * 512: ec * out_stride + (tb + 1) * 512]
                ev[it] = p.op("vector", lambda e, b=bank, c=cos_sl, d=dstq:
                              e.tensor_tensor(out=d, in0=b[:], in1=c, op=ALU.mult))
        return ev, fold_done, mm_done

    q_ev, q_folds, q_mms = proj_stream(H, wq_d, q_fm, TQ, TQ // 512, [0, 0])
    k_ev, k_folds, k_mms = proj_stream(HKV, wk_d, k_fm, NSEQ, NSEQ // 512, q_ev[-2:],
                                       prev_fold_tail=q_folds[-2:],
                                       prev_mm_tail=q_mms[-2:])

    p.barrier()
    vslots = [pA[:, 0:512], pA[:, 512:1024], pB[:, 0:512], pB[:, 512:1024],
              pC[:], pD[:], pT32[:, 0:512], pT32[:, 512:1024]]
    v_ev = {}
    wv_mm_last = 0
    for tkg in range(2):
        for half in range(2):
            src = wv_d[half * 1024:(half + 1) * 1024, :].rearrange("(dc q) j -> q dc j", q=128)
            dstv = wv_half[:].rearrange("q (dc j) -> q dc j", dc=8)
            if wv_mm_last:
                p.wait("gpsimd", "st", wv_mm_last)
            p.dma("gpsimd", "dxc", lambda e, s=src, d=dstv: e.dma_start(out=d, in_=s))
            dxc_t = p.cnt["dxc"]
            p.wait("tensor", "dxc", dxc_t)
            for tki in range(8):
                tkc = tkg * 8 + tki
                if half == 0 and tkg > 0:
                    p.wait("tensor", "sa", v_ev[tki])
                for dc8 in range(8):
                    dc = half * 8 + dc8
                    fn = (lambda e, b=vslots[tki], dc=dc, dc8=dc8, tkc=tkc:
                          e.matmul(b,
                                   x_fm[:, dc * NSEQ + tkc * 128: dc * NSEQ + tkc * 128 + 128],
                                   wv_half[:, dc8 * 512:(dc8 + 1) * 512],
                                   start=(dc == 0), stop=(dc == 15)))
                    if not (tki == 7 and dc8 == 7):
                        p.traw(fn)
                    else:
                        wv_mm_last = p.tclose(fn)
        p.wait("scalar", "st", wv_mm_last)
        for tki in range(8):
            tkc = tkg * 8 + tki
            v_ev[tki] = p.op("scalar", lambda e, b=vslots[tki], tkc=tkc:
                             e.activation(out=v_tok[:, tkc * 512:(tkc + 1) * 512],
                                          in_=b, func=AF.Copy))

    tap("q_fm", q_fm[:], [128, H * TQ], BF16)
    tap("k_fm", k_fm[:], [128, HKV * NSEQ], BF16)
    tap("v_tok", v_tok[:], [128, NKC * 512], BF16)

    p.barrier()
    pool.release(gX)

    gP = []
    sbP = lambda n, s, d: (gP.append(pool.alloc(n, s, d)) or gP[-1])
    qsq = [sbP(f"qsq{i}", [128, NSEQ], BF16) for i in range(2)]
    brow_bf = sbP("brow_bf", [1, 16], BF16)
    attnout = pool.alloc("attnout", [128, H * TQ], BF16)
    pt = sbP("pt", [128, NKC * 512], BF16)
    rinv = sbP("rinv", [1, 512], F32)
    bc_sb = sbP("bc_sb", [128, 512], F32)

    red_done = 0
    mm_g = 0
    mm_g_ring = [0, 0]
    for ec in range(H + HKV):
        is_q = ec < H
        width = TQ if is_q else NSEQ
        src_fm = q_fm if is_q else k_fm
        off = ec * TQ if is_q else (ec - H) * NSEQ
        qs_slot = qsq[ec % 2]
        if mm_g_ring[ec % 2]:
            p.wait("scalar", "st", mm_g_ring[ec % 2])
        sq_done = p.op("scalar", lambda e, s=src_fm, o=off, w=width, q=qs_slot:
                       e.activation(out=q[:, 0:w], in_=s[:, o:o + w], func=AF.Square))
        p.wait("tensor", "sa", sq_done)
        if red_done:
            p.wait("tensor", "sv", red_done)
        nblk = width // 512
        for blk in range(nblk):
            dstp = (pA if blk < 2 else pB)[0:1, (blk % 2) * 512:(blk % 2) * 512 + 512]
            fn = (lambda e, d=dstp, blk=blk, q=qs_slot: e.matmul(
                d, onescol_b[:], q[:, blk * 512:(blk + 1) * 512], start=True, stop=True))
            if blk < nblk - 1:
                p.traw(fn)
            else:
                mm_g = p.tclose(fn)
        mm_g_ring[ec % 2] = mm_g
        p.wait("vector", "st", mm_g)
        col = ec if is_q else 32 + (ec - H)
        p.op("vector", lambda e, col=col: e.reduce_max(
            out=qkrow[0:1, col:col + 1], in_=pA[0:1, 0:1024], axis=AX.X))
        if not is_q:
            p.op("vector", lambda e: e.reduce_max(
                out=qkrow[0:1, 40:41], in_=pB[0:1, 0:1024], axis=AX.X))
            red_done = p.op("vector", lambda e, col=col: e.tensor_tensor(
                out=qkrow[0:1, col:col + 1], in0=qkrow[0:1, col:col + 1],
                in1=qkrow[0:1, 40:41], op=ALU.max))
        else:
            red_done = p.cnt["sv"]
    prods = 0
    for h in range(H):
        prods = p.op("vector", lambda e, h=h: e.tensor_tensor(
            out=qkrow[0:1, 16 + h:17 + h], in0=qkrow[0:1, h:h + 1],
            in1=qkrow[0:1, 32 + h // 4:33 + h // 4], op=ALU.mult))
    p.wait("scalar", "sv", prods)
    sq2 = p.op("scalar", lambda e: e.activation(
        out=qkrow[0:1, 48:64], in_=qkrow[0:1, 16:32], func=AF.Sqrt))
    p.wait("vector", "sa", sq2)
    p.op("vector", lambda e: e.tensor_scalar_mul(out=qkrow[0:1, 48:64],
                                                 in0=qkrow[0:1, 48:64],
                                                 scalar1=-SCALE * 1.02))
    brow_done = p.op("vector", lambda e: e.tensor_copy(out=brow_bf[:], in_=qkrow[0:1, 48:64]))
    p.wait("tensor", "sv", brow_done)
    p.wait("tensor", "sv", red_done)
    gbias = p.tclose(lambda e: e.matmul(pC[:, 0:16], onesrow_b[:], brow_bf[:],
                                        start=True, stop=True))
    p.wait("scalar", "st", gbias)
    bias_done = p.op("scalar", lambda e: e.activation(out=bias_bc[:], in_=pC[:, 0:16],
                                                      func=AF.Copy))
    tap("bias_bc", bias_bc[:], [128, H], F32)

    p.barrier()
    exp_cnt = {}
    avsub = {}
    av_banks = [pC[:], pT32[:, 512:1024]]
    av_close = {}
    av_ev = {}
    rv_ev = {}
    bc_mm = {}
    bc_ev = {}
    n_it = 0

    def attn_finish(it):
        qs_f = (it // 2) * TQ + (it % 2) * 512
        p.wait("vector", "st", avsub[(it, 7)])
        rv_ev[it] = p.op("vector", lambda e: e.reciprocal(out=rinv[:], in_=pD[0:1, 0:512]))
        p.wait("tensor", "sv", rv_ev[it])
        if it > 0:
            p.wait("tensor", "sa", bc_ev[it - 1])
        bc_mm[it] = p.tclose(lambda e: e.matmul(pT32[:, 0:512], onesrow_f[:], rinv[:],
                                                start=True, stop=True))
        p.wait("scalar", "st", bc_mm[it])
        if it > 0:
            p.wait("scalar", "sv", av_ev[it - 1])
        bc_ev[it] = p.op("scalar", lambda e: e.activation(out=bc_sb[:], in_=pT32[:, 0:512],
                                                          func=AF.Copy))
        p.wait("vector", "sa", bc_ev[it])
        av_ev[it] = p.op("vector", lambda e, qs_f=qs_f, b=av_banks[it % 2]: e.tensor_tensor(
            out=attnout[:, qs_f:qs_f + 512], in0=b, in1=bc_sb[:], op=ALU.mult))

    for h in range(H):
        kv = h // 4
        for qb in range(TQ // 512):
            it = n_it
            n_it += 1
            qs = h * TQ + qb * 512
            st_g = {}
            for g in range(8):
                bank = pA if g % 2 == 0 else pB
                if (it, g - 2) in exp_cnt:
                    p.wait("tensor", "sa", exp_cnt[(it, g - 2)])
                elif it > 0 and g < 2:
                    p.wait("tensor", "sa", exp_cnt[(it - 1, 6 + g)])
                for ci in range(2):
                    c = 2 * g + ci
                    fn = (lambda e, b=bank, kv=kv, c=c, ci=ci, qs=qs:
                          e.matmul(b[:, ci * 512:(ci + 1) * 512],
                                   k_fm[:, kv * NSEQ + c * 128: kv * NSEQ + c * 128 + 128],
                                   q_fm[:, qs:qs + 512], start=True, stop=True))
                    if ci == 0:
                        p.traw(fn)
                    else:
                        st_g[g] = p.tclose(fn)
                p.wait("scalar", "st", st_g[g])
                if it > 0:
                    p.wait("scalar", "st", avsub[(it - 1, g)])
                if it == 0 and g == 0:
                    p.wait("scalar", "sa", bias_done)
                exp_cnt[(it, g)] = p.op("scalar", lambda e, b=bank, g=g, h=h: e.activation(
                    out=pt[:, 2 * g * 512: 2 * g * 512 + 1024], in_=b[:],
                    func=AF.Exp, bias=bias_bc[:, h:h + 1], scale=SCALE))
            if it > 0:
                attn_finish(it - 1)
            avb = av_banks[it % 2]
            for g in range(8):
                p.wait("tensor", "sa", exp_cnt[(it, g)])
                if g == 0:
                    if it >= 2:
                        p.wait("tensor", "sv", av_ev[it - 2])
                    if it > 0:
                        p.wait("tensor", "sv", rv_ev[it - 1])
                for ci in range(2):
                    c = 2 * g + ci
                    p.traw(lambda e, kv=kv, c=c, avb=avb: e.matmul(
                        avb,
                        v_tok[:, c * 512 + kv * 128: c * 512 + kv * 128 + 128],
                        pt[:, c * 512:(c + 1) * 512],
                        start=(c == 0), stop=(c == NKC - 1)))
                    fn = (lambda e, c=c: e.matmul(
                        pD[0:1, 0:512], onescol_b[:], pt[:, c * 512:(c + 1) * 512],
                        start=(c == 0), stop=(c == NKC - 1)))
                    if ci < 1:
                        p.traw(fn)
                    else:
                        avsub[(it, g)] = p.tclose(fn)
    attn_finish(n_it - 1)

    tap("attnout", attnout[:], [128, H * TQ], BF16)
    p.barrier()
    pool.release(gP)
    pool.release(gQKV)

    gWO = []
    sbW = lambda n, s, d: (gWO.append(pool.alloc(n, s, d)) or gWO[-1])
    x1pre = sbW("x1pre", [128, NTC * D], F32)
    wo_col = [sbW(f"wo_col{i}", [128, H * 512], BF16) for i in range(2)]
    xown = [sbW(f"xown{i}", [128, D], F32) for i in range(2)]

    wo_ev = {}
    xown_user = [0, 0]
    wo_mm_close = {}
    for db in range(4):
        src = wo_d[:, db * 512:(db + 1) * 512].rearrange("(ec q) j -> q ec j", q=128)
        dstv = wo_col[db % 2][:].rearrange("q (ec j) -> q ec j", ec=H)
        if db >= 2:
            p.wait("gpsimd", "st", wo_mm_close[db - 2])
        p.dma("gpsimd", "dxc", lambda e, s=src, d=dstv: e.dma_start(out=d, in_=s))
        dxc_t = p.cnt["dxc"]
        p.wait("tensor", "dxc", dxc_t)
        for tc in range(NTC):
            sl = xown[tc % 2]
            p.wait("sync", "sv", xown_user[tc % 2])
            p.dma("sync", "dl", lambda e, sl=sl, tc=tc:
                  e.dma_start(out=sl[:], in_=xb[tc * 128:(tc + 1) * 128, :]))
            dl_t = p.cnt["dl"]
            bank = pC if tc % 2 == 0 else pD
            it = db * NTC + tc
            if it >= 2:
                pit = it - 2
                p.wait("tensor", "sv", wo_ev[(pit // NTC, pit % NTC)])
            for ec in range(H):
                fn = (lambda e, b=bank, ec=ec, tc=tc, w=wo_col[db % 2]:
                      e.matmul(b[:], attnout[:, ec * TQ + tc * 128: ec * TQ + tc * 128 + 128],
                               w[:, ec * 512:(ec + 1) * 512],
                               start=(ec == 0), stop=(ec == H - 1)))
                if ec < H - 1:
                    p.traw(fn)
                else:
                    wo_mm_close[db] = p.tclose(fn)
            p.wait("vector", "st", wo_mm_close[db])
            p.wait("vector", "dl", dl_t)
            dst = x1pre[:, tc * D + db * 512: tc * D + (db + 1) * 512]
            wo_ev[(db, tc)] = p.op("vector", lambda e, b=bank, sl=sl, db=db, d=dst:
                                   e.tensor_tensor(out=d, in0=b[:],
                                                   in1=sl[:, db * 512:(db + 1) * 512],
                                                   op=ALU.add))
            xown_user[tc % 2] = wo_ev[(db, tc)]
            if db == 3:
                for blk in range(4):
                    p.op("vector", lambda e, tc=tc, blk=blk: e.bn_stats(
                        out=bnst[:, 6 * blk:6 * blk + 6],
                        in_=x1pre[:, tc * D + 512 * blk: tc * D + 512 * (blk + 1)]))
                p.op("vector", lambda e, tc=tc: e.bn_aggr(
                    out=bnag8[:, 2 * tc:2 * tc + 2], in_=bnst[:]))

    def layernorm(src_ap, dst_ap):
        for blk in range(4):
            p.op("vector", lambda e, blk=blk: e.bn_stats(
                out=bnst[:, 6 * blk:6 * blk + 6], in_=src_ap[:, 512 * blk:512 * (blk + 1)]))
        agg = p.op("vector", lambda e: e.bn_aggr(out=bnag[:], in_=bnst[:]))
        p.wait("scalar", "sv", agg)
        sqv = p.op("scalar", lambda e: e.activation(out=colp[:, 0:1], in_=bnag[:, 1:2],
                                                    func=AF.Sqrt, bias=epscol[:, 0:1]))
        p.wait("vector", "sa", sqv)
        p.op("vector", lambda e: e.reciprocal(out=colp[:, 1:2], in_=colp[:, 0:1]))
        nmr = p.op("vector", lambda e: e.scalar_tensor_tensor(
            out=colp[:, 2:3], in0=bnag[:, 0:1], scalar=-1.0, in1=colp[:, 1:2],
            op0=ALU.mult, op1=ALU.mult))
        p.wait("scalar", "sv", nmr)
        return p.op("scalar", lambda e: e.activation(out=dst_ap, in_=src_ap,
                                                     func=AF.Identity,
                                                     scale=colp[:, 1:2], bias=colp[:, 2:3]))

    agg8 = p.cnt["sv"]
    p.wait("scalar", "sv", agg8)
    var8 = bnag8[:].rearrange("q (a b) -> q a b", b=2)[:, :, 1:2]
    mean8 = bnag8[:].rearrange("q (a b) -> q a b", b=2)[:, :, 0:1]
    sq8 = p.op("scalar", lambda e: e.activation(
        out=lncol[:, 0:8].rearrange("q (a b) -> q a b", b=1), in_=var8,
        func=AF.Sqrt, bias=epscol[:, 0:1]))
    p.wait("vector", "sa", sq8)
    p.op("vector", lambda e: e.reciprocal(out=lncol[:, 8:16], in_=lncol[:, 0:8]))
    nmr8 = p.op("vector", lambda e: e.scalar_tensor_tensor(
        out=lncol[:, 16:24].rearrange("q (a b) -> q a b", b=1), in0=mean8, scalar=-1.0,
        in1=lncol[:, 8:16].rearrange("q (a b) -> q a b", b=1),
        op0=ALU.mult, op1=ALU.mult))
    p.wait("scalar", "sv", nmr8)
    ln1_done = [0] * NTC
    for tc in range(NTC):
        ln1_done[tc] = p.op("scalar", lambda e, tc=tc: e.activation(
            out=x1[:, tc * D:(tc + 1) * D], in_=x1pre[:, tc * D:(tc + 1) * D],
            func=AF.Identity, scale=lncol[:, 8 + tc:9 + tc], bias=lncol[:, 16 + tc:17 + tc]))

    tap("x1", x1[:], [128, NTC * D], BF16)
    p.barrier()
    pool.release([attnout])
    pool.release(gWO)

    gXQ = []
    sbq = lambda n, s, d: (gXQ.append(pool.alloc(n, s, d)) or gXQ[-1])
    xq_fm = sbq("xq_fm", [128, NDC * TQ], BF16)
    xq_tok = sbq("xq_tok", [128, D], BF16)
    sxrow = sbq("sxrow", [1, TQ], F32)
    sxbc = sbq("sxbc", [128, TQ], F32)

    h_fm = pool.alloc("h_fm", [128, NIC * TQ], BF16)
    gF1 = []
    sbf = lambda n, s, d: (gF1.append(pool.alloc(n, s, d)) or gF1[-1])
    stage = [sbf(f"stage{i}", [128, D], BF16) for i in range(2)]
    w1t = [sbf(f"w1t{i}", [128, D], BF16) for i in range(2)]
    wcols = sbf("wcols", [128, 12], F32)
    g1s = sbf("g1s", [128, 512], F32)
    sils = sbf("sils", [128, 512], F32)
    u1s = sbf("u1s", [128, 512], F32)

    gF2 = []
    sbf2 = lambda n, s, d: (gF2.append(pool.alloc(n, s, d)) or gF2[-1])
    tmax = sbf2("tmax", [128, TQ], BF16)
    tmin = sbf2("tmin", [128, TQ], BF16)

    pt_half_ev = [0, 0]
    w1_q = {}
    tr_close = {}
    w1t_ev = {}
    ffn1_ev = {}

    def ffn1_quant(osl):
        par = osl % 2
        par3 = osl % 3
        slots = [stage[par * 2], stage[par * 2 + 1]]
        for gu in range(2):
            rows = w1_d[gu * II + osl * 128: gu * II + (osl + 1) * 128, :]
            dep = tr_close.get((osl - 2, gu), 0)
            if dep:
                p.wait("gpsimd", "st", dep)
            p.dma("gpsimd", "dxc", lambda e, s=rows, d=slots[gu]: e.dma_start(out=d[:], in_=s))
            dxc_t = p.cnt["dxc"]
            p.wait("vector", "dxc", dxc_t)
            p.op("vector", lambda e, s=slots[gu], g=gu: e.tensor_scalar(
                out=s[:], in0=s[:], scalar1=-3.0e38, scalar2=None, op0=ALU.max,
                op1=ALU.max, accum_out=colp[:, 8 + g:9 + g]))
            p.op("vector", lambda e, s=slots[gu], g=gu: e.tensor_scalar(
                out=s[:], in0=s[:], scalar1=3.0e38, scalar2=None, op0=ALU.min,
                op1=ALU.min, accum_out=colp[:, 10 + g:11 + g]))
        p.op("vector", lambda e: e.tensor_scalar(
            out=colp[:, 10:12], in0=colp[:, 10:12], scalar1=-1.0, scalar2=None,
            op0=ALU.mult))
        p.op("vector", lambda e: e.tensor_tensor(out=colp[:, 8:10], in0=colp[:, 8:10],
                                                 in1=colp[:, 10:12], op=ALU.max))
        p.op("vector", lambda e: e.tensor_scalar_max(out=colp[:, 8:10], in0=colp[:, 8:10],
                                                     scalar1=AMAX_MIN))
        icols = wcols[:, par3 * 4: par3 * 4 + 2]
        swcs = wcols[:, par3 * 4 + 2: par3 * 4 + 4]
        p.op("vector", lambda e, ic=icols: e.reciprocal(out=ic, in_=colp[:, 8:10]))
        icd = p.op("vector", lambda e, ic=icols: e.tensor_scalar_mul(out=ic, in0=ic,
                                                                     scalar1=127.0))
        p.op("vector", lambda e, sw=swcs: e.tensor_scalar_mul(
            out=sw, in0=colp[:, 8:10], scalar1=1.0 / 127.0))
        p.wait("scalar", "sv", icd)
        for gu in range(2):
            td = p.op("scalar", lambda e, s=slots[gu], g=gu: e.activation(
                out=scr_a[:], in_=s[:], func=AF.Identity,
                scale=wcols[:, par3 * 4 + g: par3 * 4 + g + 1], bias=mcol[:, 0:1]))
            w1_q[(osl, gu)] = p.op("scalar", lambda e, s=slots[gu]: e.activation(
                out=s[:], in_=scr_a[:], func=AF.Identity, bias=negm[:, 0:1]))

    ffn1_quant(0)

    xq_tr = [0] * NTC
    xq_ev = [0] * NTC
    for tc in range(NTC):
        p.op("vector", lambda e, tc=tc: e.reduce_max(
            out=acol8[:, tc:tc + 1], in_=x1[:, tc * D:(tc + 1) * D],
            axis=AX.X, apply_absolute_value=True))
    p.op("vector", lambda e: e.tensor_scalar_max(out=acol8[:, 0:8], in0=acol8[:, 0:8],
                                                 scalar1=AMAX_MIN))
    p.op("vector", lambda e: e.reciprocal(out=acol8[:, 8:16], in_=acol8[:, 0:8]))
    icd = p.op("vector", lambda e: e.tensor_scalar_mul(out=acol8[:, 8:16],
                                                       in0=acol8[:, 8:16], scalar1=127.0))
    sxc = p.op("vector", lambda e: e.tensor_scalar_mul(out=acol8[:, 16:24],
                                                       in0=acol8[:, 0:8],
                                                       scalar1=1.0 / 127.0))
    p.wait("scalar", "sv", icd)
    for tc in range(NTC):
        x1t = x1[:, tc * D:(tc + 1) * D]
        if tc >= 1:
            p.wait("scalar", "sv", xq_q[tc - 1] if tc >= 1 else 0)
        tqd = p.op("scalar", lambda e, x1t=x1t, tc=tc: e.activation(
            out=scr_a[:], in_=x1t, func=AF.Identity, scale=acol8[:, 8 + tc:9 + tc],
            bias=mcol[:, 0:1]))
        p.wait("vector", "sa", tqd)
        if tc >= 1:
            p.wait("vector", "st", xq_tr[tc - 1])
        qd = p.op("vector", lambda e: e.tensor_scalar(
            out=xq_tok[:], in0=scr_a[:], scalar1=mcol[:, 0:1], scalar2=None,
            op0=ALU.subtract))
        xq_q = xq_q if tc >= 1 else [0] * NTC
        xq_q[tc] = qd
        p.wait("tensor", "sv", qd)
        if tc >= 1:
            p.wait("tensor", "sv", xq_ev[tc - 1])
        for dc in range(NDC):
            p.traw(lambda e, dc=dc: e.transpose(pT[:, dc * 128:(dc + 1) * 128],
                                                xq_tok[:, dc * 128:(dc + 1) * 128], idb[:]))
        xq_tr[tc] = p.tclose(lambda e, tc=tc: e.transpose(
            pA[0:1, tc * 128:(tc + 1) * 128], acol8[:, 16 + tc:17 + tc], idf[:]))
        p.wait("vector", "st", xq_tr[tc])
        src = pT[:].rearrange("q (dc t) -> q dc t", dc=NDC)
        dst = xq_fm[:].rearrange("q (dc t) -> q dc t", dc=NDC)[:, :, tc * 128:(tc + 1) * 128]
        xq_ev[tc] = p.op("vector", lambda e, s=src, d=dst: e.tensor_copy(out=d, in_=s))
    srow_done = p.op("vector", lambda e: e.tensor_copy(out=sxrow[:], in_=pA[0:1, 0:TQ]))
    p.wait("tensor", "sv", srow_done)
    gb1 = p.tclose(lambda e: e.matmul(pB[:, 0:512], onesrow_f[:], sxrow[0:1, 0:512],
                                      start=True, stop=True))
    gb2 = p.tclose(lambda e: e.matmul(pB[:, 512:1024], onesrow_f[:], sxrow[0:1, 512:1024],
                                      start=True, stop=True))
    p.wait("scalar", "st", gb2)
    sxbc_done = p.op("scalar", lambda e: e.activation(out=sxbc[:], in_=pB[:], func=AF.Copy))

    tap("xq_fm", xq_fm[:], [128, NDC * TQ], BF16)
    tap("sxbc", sxbc[:], [128, TQ], F32)


    xqb_sv = p.cnt["sv"]
    xqb_sa = p.cnt["sa"]
    xqb_st = p.cnt["st"]
    gXQ.remove(xq_tok); gXQ.remove(sxrow)
    pool.release([xq_tok, sxrow])
    stage.extend([pool.alloc(f"stage{i}", [128, D], BF16) for i in (2, 3)])
    gF1.extend(stage[2:4])

    def ffn1_trs(osl):
        stg = [stage[(osl % 2) * 2], stage[(osl % 2) * 2 + 1]]
        for gu in range(2):
            p.wait("tensor", "sa", w1_q[(osl, gu)])
            if osl == 0 and gu == 0:
                p.wait("tensor", "sv", xqb_sv)
                p.wait("tensor", "sa", xqb_sa)
            for half in range(2):
                bank = (2 * gu + half) % 2
                hoff = bank * 1024
                if pt_half_ev[bank]:
                    p.wait("tensor", "sa", pt_half_ev[bank])
                for j in range(8):
                    dc = half * 8 + j
                    fn = (lambda e, s=stg[gu], dc=dc, hoff=hoff, j=j: e.transpose(
                        pT[:, hoff + j * 128: hoff + (j + 1) * 128],
                        s[:, dc * 128:(dc + 1) * 128], idb[:]))
                    if j < 7:
                        p.traw(fn)
                    else:
                        tg = p.tclose(fn)
                p.wait("scalar", "st", tg)
                pt_half_ev[bank] = p.op("scalar", lambda e, d=w1t[gu], half=half, hoff=hoff:
                                        e.activation(out=d[:, half * 1024:(half + 1) * 1024],
                                                     in_=pT[:, hoff:hoff + 1024], func=AF.Copy))
            tr_close[(osl, gu)] = tg
        if osl == 0:
            tap("w1t0", w1t[0][:], [128, D], BF16)

    def ffn1_mms(osl):
        for tb in range(2):
            p.wait("tensor", "sa", ffn1_wt_ready[osl])
            prev = ffn1_ev.get((osl - 1, tb), 0)
            if prev:
                p.wait("tensor", "sv", prev)
            gmm = 0
            for which, psx in ((0, pA), (1, pB)):
                for hc in range(NDC):
                    fn = (lambda e, b=psx, w=w1t[which], hc=hc, tb=tb: e.matmul(
                        b[:, tb * 512:(tb + 1) * 512], w[:, hc * 128:(hc + 1) * 128],
                        xq_fm[:, hc * TQ + tb * 512: hc * TQ + (tb + 1) * 512],
                        start=(hc == 0), stop=(hc == NDC - 1)))
                    if not (which == 1 and hc == NDC - 1):
                        p.traw(fn)
                    else:
                        gmm = p.tclose(fn)
            ffn1_gmm[(osl, tb)] = gmm

    def ffn1_epi(osl):
        par3 = osl % 3
        for tb in range(2):
            p.wait("vector", "st", ffn1_gmm[(osl, tb)])
            if tb == 0 and osl == 0:
                p.wait("vector", "sa", sxbc_done)
            gg = p.op("vector", lambda e, tb=tb: e.tensor_tensor(
                out=g1s[:], in0=pA[:, tb * 512:(tb + 1) * 512],
                in1=sxbc[:, tb * 512:(tb + 1) * 512], op=ALU.mult))
            p.wait("scalar", "sv", gg)
            sil_done = p.op("scalar", lambda e, sw=wcols[:, par3 * 4 + 2:par3 * 4 + 3]:
                            e.activation(out=sils[:], in_=g1s[:], func=AF.Silu, scale=sw))
            p.op("vector", lambda e, tb=tb: e.tensor_tensor(
                out=u1s[:], in0=pB[:, tb * 512:(tb + 1) * 512],
                in1=sxbc[:, tb * 512:(tb + 1) * 512], op=ALU.mult))
            p.wait("vector", "sa", sil_done)
            ffn1_ev[(osl, tb)] = p.op(
                "vector", lambda e, osl=osl, tb=tb, sw=wcols[:, par3 * 4 + 3:par3 * 4 + 4]:
                e.scalar_tensor_tensor(
                    out=h_fm[:, osl * TQ + tb * 512: osl * TQ + (tb + 1) * 512],
                    in0=u1s[:], scalar=sw, in1=sils[:], op0=ALU.mult, op1=ALU.mult))
        if osl == 0:
            p.op("vector", lambda e: e.tensor_copy(out=tmax[:], in_=h_fm[:, 0:TQ]))
            p.op("vector", lambda e: e.tensor_copy(out=tmin[:], in_=h_fm[:, 0:TQ]))
        else:
            p.op("vector", lambda e, osl=osl: e.tensor_tensor(
                out=tmax[:], in0=tmax[:], in1=h_fm[:, osl * TQ:(osl + 1) * TQ], op=ALU.max))
            p.op("vector", lambda e, osl=osl: e.tensor_tensor(
                out=tmin[:], in0=tmin[:], in1=h_fm[:, osl * TQ:(osl + 1) * TQ], op=ALU.min))

    ffn1_gmm = {}
    ffn1_wt_ready = {}
    p.wait("gpsimd", "st", xqb_st)
    p.wait("gpsimd", "sv", xqb_sv)
    ffn1_quant(1)
    ffn1_trs(0)
    ffn1_wt_ready[0] = max(pt_half_ev)
    tap("w1q0", stage[0][:], [128, D], BF16)
    tap("w1q0u", stage[1][:], [128, D], BF16)
    for osl in range(NIC):
        if osl + 2 < NIC:
            ffn1_quant(osl + 2)
        ffn1_mms(osl)
        if osl + 1 < NIC:
            ffn1_trs(osl + 1)
            ffn1_wt_ready[osl + 1] = max(pt_half_ev)
        ffn1_epi(osl)

    tap("h_fm", h_fm[:], [128, NIC * TQ], BF16)
    p.barrier()
    pool.release(gXQ)
    pool.release(gF1)

    hrow = sbf2("hrow", [1, TQ], F32)
    irow = sbf2("irow", [1, TQ], F32)
    i127bc = sbf2("i127bc", [128, TQ], BF16)
    shbc = sbf2("shbc", [128, TQ], F32)
    tm_done = p.cnt["sv"]
    p.wait("tensor", "sv", tm_done)
    for tc in range(NTC):
        p.traw(lambda e, tc=tc: e.transpose(
            pT[:, tc * 128:(tc + 1) * 128], tmax[:, tc * 128:(tc + 1) * 128], idb[:]))
        fn = (lambda e, tc=tc: e.transpose(
            pT[:, 1024 + tc * 128: 1024 + (tc + 1) * 128],
            tmin[:, tc * 128:(tc + 1) * 128], idb[:]))
        if tc < NTC - 1:
            p.traw(fn)
        else:
            t3g = p.tclose(fn)
    p.wait("vector", "st", t3g)
    micro = 0
    for tc in range(NTC):
        if micro:
            p.wait("vector", "st", micro)
        p.op("vector", lambda e, tc=tc: e.reduce_max(
            out=colp[:, 5:6], in_=pT[:, tc * 128:(tc + 1) * 128], axis=AX.X,
            apply_absolute_value=True))
        p.op("vector", lambda e, tc=tc: e.reduce_max(
            out=colp[:, 6:7], in_=pT[:, 1024 + tc * 128: 1024 + (tc + 1) * 128], axis=AX.X,
            apply_absolute_value=True))
        p.op("vector", lambda e: e.tensor_tensor(out=colp[:, 5:6], in0=colp[:, 5:6],
                                                 in1=colp[:, 6:7], op=ALU.max))
        p.op("vector", lambda e: e.tensor_scalar_max(out=colp[:, 5:6], in0=colp[:, 5:6],
                                                     scalar1=AMAX_MIN))
        amc = p.op("vector", lambda e: e.tensor_copy(out=amaxf[:], in_=colp[:, 5:6]))
        p.wait("tensor", "sv", amc)
        micro = p.tclose(lambda e, tc=tc: e.transpose(
            pA[0:1, tc * 128:(tc + 1) * 128], amaxf[:], idf[:]))
    p.wait("vector", "st", micro)
    p.op("vector", lambda e: e.tensor_copy(out=hrow[:], in_=pA[0:1, 0:TQ]))
    p.op("vector", lambda e: e.reciprocal(out=irow[:], in_=hrow[:]))
    ird = p.op("vector", lambda e: e.tensor_scalar_mul(out=irow[:], in0=irow[:],
                                                       scalar1=127.0))
    shd = p.op("vector", lambda e: e.tensor_scalar_mul(out=hrow[:], in0=hrow[:],
                                                       scalar1=1.0 / 127.0))
    p.wait("tensor", "sv", ird)
    gb1 = p.tclose(lambda e: e.matmul(pB[:, 0:512], onesrow_f[:], irow[0:1, 0:512],
                                      start=True, stop=True))
    gb2 = p.tclose(lambda e: e.matmul(pB[:, 512:1024], onesrow_f[:], irow[0:1, 512:1024],
                                      start=True, stop=True))
    p.wait("scalar", "st", gb2)
    ibc_done = p.op("scalar", lambda e: e.activation(out=i127bc[:], in_=pB[:], func=AF.Copy))
    p.wait("tensor", "sa", ibc_done)
    p.wait("tensor", "sv", shd)
    gb3 = p.tclose(lambda e: e.matmul(pB[:, 0:512], onesrow_f[:], hrow[0:1, 0:512],
                                      start=True, stop=True))
    gb4 = p.tclose(lambda e: e.matmul(pB[:, 512:1024], onesrow_f[:], hrow[0:1, 512:1024],
                                      start=True, stop=True))
    p.wait("scalar", "st", gb4)
    shbc_done = p.op("scalar", lambda e: e.activation(out=shbc[:], in_=pB[:], func=AF.Copy))

    gF2.remove(tmax); gF2.remove(tmin); gF2.remove(hrow); gF2.remove(irow)
    pool.release([tmax, tmin, hrow, irow])
    p.barrier()
    gF3 = []
    sbf3 = lambda n, s, d: (gF3.append(pool.alloc(n, s, d)) or gF3[-1])
    outfm = pool.alloc("outfm", [128, NDC * TQ], BF16)
    stage2 = [sbf3("stage2_0", [128, II], BF16)]
    w2t = sbf3("w2t", [128, II], BF16)
    wcols2 = sbf3("wcols2", [128, 8], F32)
    o1s = sbf3("o1s", [128, 512], F32)
    pt2_half_ev = list(pt_half_ev)
    w2_q = {}
    w2_tr_last = {}
    ffn2_ev = {}

    def ffn2_quant(oc):
        par = oc % 2
        slot2 = stage2[0]
        dep = w2_tr_last.get(oc - 1, 0)
        if dep:
            p.wait("gpsimd", "st", dep)
        p.dma("gpsimd", "dxc", lambda e, oc=oc, s=slot2: e.dma_start(
            out=s[:], in_=w2_d[oc * 128:(oc + 1) * 128, :]))
        dxc_t = p.cnt["dxc"]
        icol = wcols2[:, par * 2: par * 2 + 1]
        swc = wcols2[:, 4 + par: 5 + par]
        p.wait("vector", "dxc", dxc_t)
        p.op("vector", lambda e, s=slot2: e.reduce_max(out=colp[:, 8:9], in_=s[:], axis=AX.X,
                                                       apply_absolute_value=True))
        p.op("vector", lambda e: e.tensor_scalar_max(out=colp[:, 8:9], in0=colp[:, 8:9],
                                                     scalar1=AMAX_MIN))
        p.op("vector", lambda e, ic=icol: e.reciprocal(out=ic, in_=colp[:, 8:9]))
        icd = p.op("vector", lambda e, ic=icol: e.tensor_scalar_mul(out=ic, in0=ic,
                                                                    scalar1=127.0))
        p.op("vector", lambda e, sw=swc: e.tensor_scalar_mul(out=sw, in0=colp[:, 8:9],
                                                             scalar1=1.0 / 127.0))
        prev_q = icd
        for quarter in range(4):
            w0 = quarter * (II // 4)
            p.wait("scalar", "sv", prev_q)
            td = p.op("scalar", lambda e, w0=w0, ic=icol, s=slot2: e.activation(
                out=scr_a[:, 0:II // 4], in_=s[:, w0:w0 + II // 4],
                func=AF.Identity, scale=ic, bias=mcol[:, 0:1]))
            p.wait("vector", "sa", td)
            w2_q[oc] = prev_q = p.op("vector", lambda e, w0=w0, s=slot2: e.tensor_scalar(
                out=s[:, w0:w0 + II // 4], in0=scr_a[:, 0:II // 4],
                scalar1=mcol[:, 0:1], scalar2=None, op0=ALU.subtract))

    def ffn2_trs(oc):
        for batch in range(6):
            b0 = batch * 8
            bn = 8 if batch < 5 else 4
            bank = batch % 2
            hoff = bank * 1024
            if pt2_half_ev[bank]:
                p.wait("tensor", "sa", pt2_half_ev[bank])
            if oc == 0 and batch == 0:
                p.wait("tensor", "sv", w2_q[0])
            elif batch == 0:
                p.wait("tensor", "sv", w2_q[oc])
            for j in range(bn):
                icx = b0 + j
                fn = (lambda e, icx=icx, j=j, hoff=hoff, s=stage2[0]: e.transpose(
                    pT[:, hoff + j * 128: hoff + (j + 1) * 128],
                    s[:, icx * 128:(icx + 1) * 128], idb[:]))
                if j < bn - 1:
                    p.traw(fn)
                else:
                    tg = p.tclose(fn)
            p.wait("scalar", "st", tg)
            pt2_half_ev[bank] = p.op("scalar", lambda e, b0=b0, bn=bn, hoff=hoff:
                                     e.activation(out=w2t[:, b0 * 128:(b0 + bn) * 128],
                                                  in_=pT[:, hoff:hoff + bn * 128],
                                                  func=AF.Copy))
            w2_tr_last[oc] = tg
        ffn2_wt_ready[oc] = max(pt2_half_ev)
        if oc == 0:
            tap("w2t0", w2t[:], [128, II], BF16)

    def ffn2_mms(oc):
        for tb in range(2):
            p.wait("tensor", "sa", ffn2_wt_ready[oc])
            if oc == 0 and tb == 1:
                p.wait("tensor", "sv", hqb_sv)
            prev = ffn2_ev.get((oc - 1, tb), 0)
            if prev:
                p.wait("tensor", "sv", prev)
            for icx in range(NIC):
                if oc == 0 and tb == 0 and icx % 11 == 0:
                    p.wait("tensor", "sv", hq_done[min(icx + 10, NIC - 1)])
                fn = (lambda e, icx=icx, tb=tb: e.matmul(
                    pA[:, tb * 512:(tb + 1) * 512], w2t[:, icx * 128:(icx + 1) * 128],
                    h_fm[:, icx * TQ + tb * 512: icx * TQ + (tb + 1) * 512],
                    start=(icx == 0), stop=(icx == NIC - 1)))
                if icx < NIC - 1:
                    p.traw(fn)
                else:
                    gmm = p.tclose(fn)
            ffn2_gmm[(oc, tb)] = gmm

    def ffn2_epi(oc):
        par = oc % 2
        for tb in range(2):
            p.wait("vector", "st", ffn2_gmm[(oc, tb)])
            if oc == 0 and tb == 0:
                p.wait("vector", "sa", shbc_done)
            p.op("vector", lambda e, tb=tb: e.tensor_tensor(
                out=o1s[:], in0=pA[:, tb * 512:(tb + 1) * 512],
                in1=shbc[:, tb * 512:(tb + 1) * 512], op=ALU.mult))
            ffn2_ev[(oc, tb)] = p.op(
                "vector", lambda e, oc=oc, tb=tb, sw=wcols2[:, 4 + par:5 + par]:
                e.tensor_scalar(
                    out=outfm[:, oc * TQ + tb * 512: oc * TQ + (tb + 1) * 512],
                    in0=o1s[:], scalar1=sw, scalar2=None, op0=ALU.mult))

    ffn2_gmm = {}
    ffn2_wt_ready = {}
    ffn2_quant(0)

    hq_done = [0] * NIC
    p.wait("vector", "sa", ibc_done)
    for ic in range(NIC):
        p.op("vector", lambda e, ic=ic: e.tensor_tensor(
            out=scr_a[:, 0:TQ], in0=h_fm[:, ic * TQ:(ic + 1) * TQ], in1=i127bc[:],
            op=ALU.mult))
        hq_done[ic] = p.op("vector", lambda e, ic=ic: e.tensor_scalar(
            out=h_fm[:, ic * TQ:(ic + 1) * TQ], in0=scr_a[:, 0:TQ],
            scalar1=mcol[:, 0:1], scalar2=mcol[:, 0:1], op0=ALU.add, op1=ALU.subtract))

    tap("hq_fm", h_fm[:], [128, NIC * TQ], BF16)
    tap("shbc", shbc[:], [128, TQ], F32)

    hqb_sv = p.cnt["sv"]

    ffn2_trs(0)
    for oc in range(NDC):
        if oc + 1 < NDC:
            ffn2_quant(oc + 1)
        ffn2_mms(oc)
        if oc + 1 < NDC:
            ffn2_trs(oc + 1)
        ffn2_epi(oc)

    tap("outfm", outfm[:], [128, NDC * TQ], BF16)
    p.barrier()
    pool.release([h_fm])
    pool.release(gF2)
    pool.release(gF3)

    yln8 = pool.alloc("yln8", [128, NTC * D], F32)
    fin_tr = [0] * NTC
    fin_y = [0] * NTC
    st_out = [0] * NTC
    for tc in range(NTC):
        if tc >= 1:
            p.wait("tensor", "sv", fin_y[tc - 1])
        for oc in range(NDC):
            fn = (lambda e, oc=oc, tc=tc: e.transpose(
                pT[:, oc * 128:(oc + 1) * 128],
                outfm[:, oc * TQ + tc * 128: oc * TQ + tc * 128 + 128], idb[:]))
            if oc < NDC - 1:
                p.traw(fn)
            else:
                fin_tr[tc] = p.tclose(fn)
        p.wait("vector", "st", fin_tr[tc])
        fin_y[tc] = p.op("vector", lambda e, tc=tc: e.scalar_tensor_tensor(
            out=yln8[:, tc * D:(tc + 1) * D], in0=pT[:], scalar=1.0,
            in1=x1[:, tc * D:(tc + 1) * D], op0=ALU.mult, op1=ALU.add))
        for blk in range(4):
            p.op("vector", lambda e, tc=tc, blk=blk: e.bn_stats(
                out=bnst[:, 6 * blk:6 * blk + 6],
                in_=yln8[:, tc * D + 512 * blk: tc * D + 512 * (blk + 1)]))
        p.op("vector", lambda e, tc=tc: e.bn_aggr(
            out=bnag8[:, 2 * tc:2 * tc + 2], in_=bnst[:]))
    agg8b = p.cnt["sv"]
    p.wait("scalar", "sv", agg8b)
    var8b = bnag8[:].rearrange("q (a b) -> q a b", b=2)[:, :, 1:2]
    mean8b = bnag8[:].rearrange("q (a b) -> q a b", b=2)[:, :, 0:1]
    sq8b = p.op("scalar", lambda e: e.activation(
        out=lncol[:, 0:8].rearrange("q (a b) -> q a b", b=1), in_=var8b,
        func=AF.Sqrt, bias=epscol[:, 0:1]))
    p.wait("vector", "sa", sq8b)
    p.op("vector", lambda e: e.reciprocal(out=lncol[:, 8:16], in_=lncol[:, 0:8]))
    nmr8b = p.op("vector", lambda e: e.scalar_tensor_tensor(
        out=lncol[:, 16:24].rearrange("q (a b) -> q a b", b=1), in0=mean8b, scalar=-1.0,
        in1=lncol[:, 8:16].rearrange("q (a b) -> q a b", b=1),
        op0=ALU.mult, op1=ALU.mult))
    p.wait("scalar", "sv", nmr8b)
    for tc in range(NTC):
        ln2 = p.op("scalar", lambda e, tc=tc: e.activation(
            out=yln8[:, tc * D:(tc + 1) * D], in_=yln8[:, tc * D:(tc + 1) * D],
            func=AF.Identity, scale=lncol[:, 8 + tc:9 + tc], bias=lncol[:, 16 + tc:17 + tc]))
        p.wait("sync", "sa", ln2)
        st_out[tc] = p.dma("sync", "dso", lambda e, tc=tc: e.dma_start(
            out=out_d[tc * 128:(tc + 1) * 128, :], in_=yln8[:, tc * D:(tc + 1) * D]))
    p.wait("sync", "dso", p.cnt["dso"])

    sems = {}
    semctx = ExitStack()
    for s in ALL_SEMS:
        sems[s] = semctx.enter_context(nc.semaphore(s))

    def run_stream(eng, items):
        for it in items:
            if it[0] == "w":
                eng.wait_ge(sems[it[1]], it[2])
            elif it[0] == "i":
                inst = it[1](eng)
                inst.then_inc(sems[it[2]], it[3])
            else:
                it[1](eng)

    pool.finalize(ctx)
    with ctx, semctx, nc.Block() as block:
        block.sync(lambda e: run_stream(e, p.stream["sync"]))
        block.scalar(lambda e: run_stream(e, p.stream["scalar"]))
        block.vector(lambda e: run_stream(e, p.stream["vector"]))
        block.tensor(lambda e: run_stream(e, p.stream["tensor"]))
        block.gpsimd(lambda e: run_stream(e, p.stream["gpsimd"]))
    return nc


def _cos_table():
    j = np.arange(0, HD, 2, dtype=np.float64)
    inv_freq = 1.0 / (10000.0 ** (j / HD))
    t = np.arange(NSEQ, dtype=np.float64)
    freqs = t[:, None] * inv_freq[None, :]
    emb = np.concatenate([freqs, freqs], axis=-1)
    return np.cos(emb).T.astype(np.float32)


def _consts():
    return {
        "idb": np.eye(128, dtype=ml_dtypes.bfloat16),
        "idf": np.eye(128, dtype=np.float32),
        "onescol_b": np.ones((128, 1), dtype=ml_dtypes.bfloat16),
        "onesrow_f": np.ones((1, 128), dtype=np.float32),
        "onesrow_b": np.ones((1, 128), dtype=ml_dtypes.bfloat16),
    }


def make_in_maps(x, wq, wk, wv, wo, w1, w2):
    cos = _cos_table()
    consts = _consts()
    in_maps = []
    for core in range(8):
        b, half = core // 2, core % 2
        own = slice(half * TQ, (half + 1) * TQ)
        oth = slice((1 - half) * TQ, (2 - half) * TQ)
        xbp = np.ascontiguousarray(np.concatenate([x[b][own], x[b][oth]], axis=0))
        coskp = np.ascontiguousarray(np.concatenate([cos[:, own], cos[:, oth]], axis=1))
        m = {"xb": xbp, "cosk": coskp, "wq": wq, "wk": wk, "wv": wv, "wo": wo,
             "w1": w1, "w2": w2}
        m.update(consts)
        in_maps.append(m)
    return in_maps


def kernel(L0_attn_x, L0_attn_Wq, L0_attn_Wk, L0_attn_Wv, L0_attn_Wo,
           L0_ffn_w_gate_up, L0_ffn_w_down):
    x = np.ascontiguousarray(np.asarray(L0_attn_x, dtype=np.float32))
    wq = np.ascontiguousarray(np.asarray(L0_attn_Wq, dtype=np.float32))
    wk = np.ascontiguousarray(np.asarray(L0_attn_Wk, dtype=np.float32))
    wv = np.ascontiguousarray(np.asarray(L0_attn_Wv, dtype=np.float32))
    wo = np.ascontiguousarray(np.asarray(L0_attn_Wo, dtype=np.float32))
    w1 = np.ascontiguousarray(np.asarray(L0_ffn_w_gate_up, dtype=np.float32))
    w2 = np.ascontiguousarray(np.asarray(L0_ffn_w_down, dtype=np.float32))

    nc = bass.Bass("TRN2", target_bir_lowering=False)
    build_program(nc)
    in_maps = make_in_maps(x, wq, wk, wv, wo, w1, w2)
    res = run_bass_kernel_spmd(nc, in_maps, core_ids=list(range(8)))

    out = np.empty((4, NSEQ, D), dtype=np.float32)
    for core in range(8):
        b, half = core // 2, core % 2
        out[b, half * TQ:(half + 1) * TQ] = res.results[core]["out"]
    return out


# version 16
# speedup vs baseline: 1.1699x; 1.0557x over previous
import numpy as np
import ml_dtypes
from contextlib import ExitStack

import concourse.bass as bass
import concourse.mybir as mybir
from concourse.bass_utils import run_bass_kernel_spmd

F32 = mybir.dt.float32
BF16 = mybir.dt.bfloat16
AF = mybir.ActivationFunctionType
ALU = mybir.AluOpType
AX = mybir.AxisListType

D = 2048
NSEQ = 2048
H = 16
HKV = 4
HD = 128
II = 5632
TQ = 1024
SCALE = float(HD) ** -0.5
C52 = 1.5 * 2.0 ** 52
C32 = 1.5 * 2.0 ** 23
EPS_LN = 1e-5
AMAX_MIN = 127.0e-8
NDC = D // 128
NIC = II // 128
NTC = TQ // 128
NKC = NSEQ // 128

ALL_SEMS = ("sv", "sa", "sp", "st", "dxc", "dl", "dso", "dsc")


class Prog:
    ENGS = ("sync", "scalar", "vector", "tensor", "gpsimd")
    SELF = {"vector": "sv", "scalar": "sa", "gpsimd": "sp", "tensor": "st"}

    def __init__(self):
        self.stream = {e: [] for e in self.ENGS}
        self.cnt = {s: 0 for s in ALL_SEMS}

    def wait(self, eng, sem, val):
        if val > 0:
            self.stream[eng].append(("w", sem, val))

    def op(self, eng, fn):
        s = self.SELF[eng]
        self.stream[eng].append(("w", s, self.cnt[s]))
        self.cnt[s] += 1
        self.stream[eng].append(("i", fn, s, 1))
        return self.cnt[s]

    def traw(self, fn):
        self.stream["tensor"].append(("n", fn))

    def tclose(self, fn):
        self.cnt["st"] += 1
        self.stream["tensor"].append(("i", fn, "st", 1))
        return self.cnt["st"]

    def dma(self, eng, sem, fn):
        self.cnt[sem] += 16
        self.stream[eng].append(("i", fn, sem, 16))
        return self.cnt[sem]

    def barrier(self):
        snap = dict(self.cnt)
        for e in self.ENGS:
            own = self.SELF.get(e)
            for s in ALL_SEMS:
                if s != own:
                    self.wait(e, s, snap[s])




class SbufPool:

    def __init__(self, nc):
        self.nc = nc
        self.free = [(0, 1 << 30)]
        self.hwm = 0
        self.items = []
        self.where = {}

    def alloc(self, name, shape, dtype):
        pp = int(np.prod(shape[1:])) * mybir.dt.size(dtype)
        pp = (pp + 31) // 32 * 32
        cands = [(l, s, i) for i, (s, l) in enumerate(self.free) if l >= pp]
        if not cands:
            raise MemoryError(f"SbufPool: no slot for {name} ({pp}B)")
        l, s, i = min(cands)
        addr = s
        if l == pp:
            self.free.pop(i)
        else:
            self.free[i] = (s + pp, l - pp)
        h = self.nc.virtual_sbuf_tensor("v_" + name, shape, dtype)
        self.items.append((h, addr))
        self.where[h.name] = (addr, pp)
        self.hwm = max(self.hwm, addr + pp)
        return h

    def release(self, handles):
        for h in handles:
            addr, size = self.where.pop(h.name)
            self.free.append((addr, size))
        self.free.sort()
        merged = []
        for s, l in self.free:
            if merged and merged[-1][0] + merged[-1][1] == s:
                merged[-1][1] += l
            else:
                merged.append([s, l])
        self.free = [(s, l) for s, l in merged]

    def finalize(self, ctx):
        arena = ctx.enter_context(
            self.nc.sbuf_tensor("sbuf_pool_arena", [128, self.hwm], mybir.dt.uint8))
        base = self.nc.lookup_mloc(arena).addr
        for h, addr in self.items:
            self.nc._tensor(h.name, h.shape, h.dtype, type="SB")
            mloc = self.nc.lookup_mloc(h)
            mloc.allocated = not self.nc.target_bir_lowering
            mloc.addr = base + addr


def build_program(nc, dbg=None):
    dbg = list(dbg or ())
    p = Prog()
    ctx = ExitStack()

    xb = nc.dram_tensor("xb", [NSEQ, D], F32, kind="ExternalInput").ap()
    cosk_d = nc.dram_tensor("cosk", [128, NSEQ], F32, kind="ExternalInput").ap()
    wq_d = nc.dram_tensor("wq", [D, H * HD], F32, kind="ExternalInput").ap()
    wk_d = nc.dram_tensor("wk", [D, HKV * HD], F32, kind="ExternalInput").ap()
    wv_d = nc.dram_tensor("wv", [D, HKV * HD], F32, kind="ExternalInput").ap()
    wo_d = nc.dram_tensor("wo", [H * HD, D], F32, kind="ExternalInput").ap()
    w1_d = nc.dram_tensor("w1", [2 * II, D], F32, kind="ExternalInput").ap()
    w2_d = nc.dram_tensor("w2", [D, II], F32, kind="ExternalInput").ap()
    idb_d = nc.dram_tensor("idb", [128, 128], BF16, kind="ExternalInput").ap()
    idf_d = nc.dram_tensor("idf", [128, 128], F32, kind="ExternalInput").ap()
    onescol_d = nc.dram_tensor("onescol_b", [128, 1], BF16, kind="ExternalInput").ap()
    onesrow_f_d = nc.dram_tensor("onesrow_f", [1, 128], F32, kind="ExternalInput").ap()
    onesrow_b_d = nc.dram_tensor("onesrow_b", [1, 128], BF16, kind="ExternalInput").ap()
    out_d = nc.dram_tensor("out", [TQ, D], F32, kind="ExternalOutput").ap()
    w1q_scr = nc.dram_tensor("w1q_scr", [2 * II, D], BF16, kind="Internal").ap()
    w2q_scr = nc.dram_tensor("w2q_scr", [D, II], BF16, kind="Internal").ap()

    def tap(name, ap, shape, dtype):
        if name not in dbg:
            return
        t = nc.dram_tensor("dbg_" + name, shape, dtype, kind="ExternalOutput").ap()
        p.barrier()
        p.dma("sync", "dso", lambda e, a=ap, t=t: e.dma_start(out=t, in_=a))

    pool = SbufPool(nc)
    sb = lambda n, s, d: pool.alloc(n, s, d)

    pA = ctx.enter_context(nc.psum_tensor("pA", [128, 1024], F32))
    pB = ctx.enter_context(nc.psum_tensor("pB", [128, 1024], F32))
    pT = ctx.enter_context(nc.psum_tensor("pT", [128, 2048], BF16))
    pC = ctx.enter_context(nc.psum_tensor("pC", [128, 512], F32))
    pD = ctx.enter_context(nc.psum_tensor("pD", [128, 512], F32))
    pT32 = pT[:].bitcast(F32)

    idb = sb("idb", [128, 128], BF16)
    idf = sb("idf", [128, 128], F32)
    onescol_b = sb("onescol_b", [128, 1], BF16)
    onesrow_f = sb("onesrow_f", [1, 128], F32)
    onesrow_b = sb("onesrow_b", [1, 128], BF16)
    mcol = sb("mcol", [128, 1], F32)
    negm = sb("negm", [128, 1], F32)
    epscol = sb("epscol", [128, 1], F32)
    bias_bc = sb("bias_bc", [128, H], F32)
    colp = sb("colp", [128, 16], F32)
    qkrow = sb("qkrow", [1, 64], F32)
    scr_a = sb("scr_a", [128, D], F32)
    bnst = sb("bnst", [128, 24], F32)
    bnag = sb("bnag", [128, 2], F32)
    bnag8 = sb("bnag8", [128, 16], F32)
    lncol = sb("lncol", [128, 24], F32)
    acol8 = sb("acol8", [128, 24], F32)
    amaxf = sb("amaxf", [128, 1], F32)
    swall = sb("swall", [128, 104], F32)
    tstg = [sb(f"tstg{i}", [128, D], BF16) for i in range(2)]
    w2stg = None

    for dst, src in ((idb, idb_d), (idf, idf_d), (onescol_b, onescol_d),
                     (onesrow_f, onesrow_f_d), (onesrow_b, onesrow_b_d)):
        p.dma("sync", "dl", lambda e, d=dst, s=src: e.dma_start(out=d[:], in_=s))
    dl_consts = p.cnt["dl"]

    p.op("vector", lambda e: e.memset(mcol[:], 1.0))
    p.op("vector", lambda e: e.tensor_scalar(out=mcol[:], in0=mcol[:], scalar1=C52,
                                             scalar2=-C52, op0=ALU.add, op1=ALU.add))
    p.op("vector", lambda e: e.tensor_scalar(out=mcol[:], in0=mcol[:], scalar1=(C52 - C32),
                                             scalar2=C32, op0=ALU.mult, op1=ALU.add))
    p.op("vector", lambda e: e.memset(epscol[:], EPS_LN))
    p.op("vector", lambda e: e.tensor_scalar_mul(out=negm[:], in0=mcol[:], scalar1=-1.0))

    _ORDER = []
    _w1i = _w2i = 0
    for _j in range(104):
        if (_j >= 26 and _j % 5 == 4 and _w2i < 16) or _w1i >= 88:
            _ORDER.append(("w2", _w2i)); _w2i += 1
        else:
            _ORDER.append(("w1", _w1i)); _w1i += 1
    tk = {"issue": 0, "done": 0, "wr": {}, "stg_wr": [0, 0], "w2_wr": 0,
          "dxc": {}, "w1n": 0}

    def task_issue(j):
        kind, i = _ORDER[j]
        if kind == "w1":
            slot = tstg[tk["w1n"] % 2]
            tk["slot%d" % j] = (slot, tk["w1n"] % 2)
            tk["w1n"] += 1
            rows = w1_d[i * 128:(i + 1) * 128, :]
            dep = tk["stg_wr"][(tk["w1n"] - 1) % 2]
        else:
            slot = w2stg
            tk["slot%d" % j] = (slot, -1)
            rows = w2_d[i * 128:(i + 1) * 128, :]
            dep = tk["w2_wr"]
        if dep:
            p.wait("gpsimd", "dsc", dep)
        p.dma("gpsimd", "dxc", lambda e, s=rows, d=slot: e.dma_start(out=d[:], in_=s))
        tk["dxc"][j] = p.cnt["dxc"]

    def task_compute(j):
        kind, i = _ORDER[j]
        slot, ring = tk["slot%d" % j]
        width = D if kind == "w1" else II
        nq = 1 if kind == "w1" else 4
        swi = i if kind == "w1" else 88 + i
        dst_scr = (w1q_scr if kind == "w1" else w2q_scr)[i * 128:(i + 1) * 128, :]
        p.wait("vector", "dxc", tk["dxc"][j])
        p.op("vector", lambda e, s=slot: e.reduce_max(
            out=colp[:, 8:9], in_=s[:], axis=AX.X, apply_absolute_value=True))
        p.op("vector", lambda e: e.tensor_scalar_max(out=colp[:, 8:9], in0=colp[:, 8:9],
                                                     scalar1=AMAX_MIN))
        p.op("vector", lambda e: e.reciprocal(out=colp[:, 9:10], in_=colp[:, 8:9]))
        p.op("vector", lambda e: e.tensor_scalar_mul(out=colp[:, 9:10],
                                                     in0=colp[:, 9:10], scalar1=127.0))
        p.op("vector", lambda e, swi=swi: e.tensor_scalar_mul(
            out=swall[:, swi:swi + 1], in0=colp[:, 8:9], scalar1=1.0 / 127.0))
        qd = 0
        for qi in range(nq):
            w0 = qi * (width // nq)
            wl = width // nq
            p.op("vector", lambda e, s=slot, w0=w0, wl=wl: e.tensor_scalar(
                out=scr_a[:, 0:wl], in0=s[:, w0:w0 + wl], scalar1=colp[:, 9:10],
                scalar2=mcol[:, 0:1], op0=ALU.mult, op1=ALU.add))
            qd = p.op("vector", lambda e, s=slot, w0=w0, wl=wl: e.tensor_scalar(
                out=s[:, w0:w0 + wl], in0=scr_a[:, 0:wl], scalar1=mcol[:, 0:1],
                scalar2=None, op0=ALU.subtract))
        p.wait("sync", "sv", qd)
        wr = p.dma("sync", "dsc", lambda e, s=slot, d=dst_scr: e.dma_start(out=d, in_=s[:]))
        tk["wr"][(kind, i)] = wr
        if kind == "w1":
            tk["stg_wr"][ring] = wr
        else:
            tk["w2_wr"] = wr

    def pump_tasks(n=1):
        for _ in range(n):
            if tk["issue"] < 104 and tk["issue"] - tk["done"] < 2:
                task_issue(tk["issue"]); tk["issue"] += 1
            lag = 0 if tk["issue"] == 104 else 1
            if tk["done"] < tk["issue"] - lag:
                task_compute(tk["done"]); tk["done"] += 1

    gX = []
    sbX = lambda n, s, d: (gX.append(pool.alloc(n, s, d)) or gX[-1])
    cosk_sb = sbX("cosk_sb", [128, NSEQ], F32)
    x_fm = sbX("x_fm", [128, NDC * NSEQ], BF16)
    ring = [sbX(f"ring{i}", [128, D], BF16) for i in range(2)]
    wfold = [sbX(f"wfold{i}", [128, D], BF16) for i in range(2)]
    wv_half = sbX("wv_half", [128, 8 * 512], BF16)

    p.dma("sync", "dl", lambda e: e.dma_start(out=cosk_sb[:], in_=cosk_d))
    dl_cos = p.cnt["dl"]

    tr_done = [0] * NKC
    evac_done = [0] * NKC
    for tc in range(NKC):
        slot = ring[tc % 2]
        if tc >= 2:
            p.wait("gpsimd", "st", tr_done[tc - 2])
        p.dma("gpsimd", "dxc",
              lambda e, s=slot, tc=tc: e.dma_start(out=s[:], in_=xb[tc * 128:(tc + 1) * 128, :]))
        dxc_t = p.cnt["dxc"]
        p.wait("tensor", "dxc", dxc_t)
        if tc == 0:
            p.wait("tensor", "dl", dl_consts)
        if tc >= 1:
            p.wait("tensor", "sv", evac_done[tc - 1])
        for dc in range(NDC):
            fn = (lambda e, s=slot, dc=dc:
                  e.transpose(pT[:, dc * 128:(dc + 1) * 128],
                              s[:, dc * 128:(dc + 1) * 128], idb[:]))
            if dc < NDC - 1:
                p.traw(fn)
            else:
                tr_done[tc] = p.tclose(fn)
        p.wait("vector", "st", tr_done[tc])
        src = pT[:].rearrange("q (dc t) -> q dc t", dc=NDC)
        dst = x_fm[:].rearrange("q (dc t) -> q dc t", dc=NDC)[:, :, tc * 128:(tc + 1) * 128]
        evac_done[tc] = p.op("vector", lambda e, s=src, d=dst: e.tensor_copy(out=d, in_=s))

    tap("x_fm", x_fm[:], [128, NDC * NSEQ], BF16)

    gQKV = []
    sbQ = lambda n, s, d: (gQKV.append(pool.alloc(n, s, d)) or gQKV[-1])
    q_fm = sbQ("q_fm", [128, H * TQ], BF16)
    k_fm = sbQ("k_fm", [128, HKV * NSEQ], BF16)
    v_tok = sbQ("v_tok", [128, NKC * 512], BF16)

    p.barrier()

    def proj_stream(n_cols, w_dram, out_fm, out_stride, n_tb, prev_tail,
                    prev_fold_tail=(0, 0), prev_mm_tail=(0, 0)):
        psr = [pC, pD]
        fold_done = [0] * (n_cols + 1)
        mm_done = [0] * n_cols
        ev = [0] * (n_cols * n_tb)

        def emit_load_fold(ec):
            cslot, fslot = ring[ec % 2], wfold[ec % 2]
            if ec >= 2:
                p.wait("gpsimd", "sv", fold_done[ec - 2])
            else:
                p.wait("gpsimd", "sv", prev_fold_tail[ec])
            src = w_dram[:, ec * 128:(ec + 1) * 128].rearrange("(dc q) j -> q dc j", q=128)
            dstv = cslot[:].rearrange("q (dc j) -> q dc j", dc=NDC)
            p.dma("gpsimd", "dxc", lambda e, s=src, d=dstv: e.dma_start(out=d, in_=s))
            dxc_t = p.cnt["dxc"]
            p.wait("vector", "dxc", dxc_t)
            if ec >= 2:
                p.wait("vector", "st", mm_done[ec - 2])
            else:
                p.wait("vector", "st", prev_mm_tail[ec])
            a = cslot[:].rearrange("q (dc j) -> q dc j", dc=NDC)
            f = fslot[:].rearrange("q (dc j) -> q dc j", dc=NDC)
            p.op("vector", lambda e, a=a, f=f: e.tensor_tensor(
                out=f[:, :, 0:64], in0=a[:, :, 0:64], in1=a[:, :, 64:128], op=ALU.subtract))
            fold_done[ec] = p.op("vector", lambda e, a=a, f=f: e.tensor_tensor(
                out=f[:, :, 64:128], in0=a[:, :, 64:128], in1=a[:, :, 0:64], op=ALU.add))

        emit_load_fold(0)
        for ec in range(n_cols):
            if ec + 1 < n_cols:
                emit_load_fold(ec + 1)
            fslot = wfold[ec % 2]
            for tb in range(n_tb):
                it = ec * n_tb + tb
                bank = psr[it % 2]
                prev = ev[it - 2] if it >= 2 else prev_tail[it]
                p.wait("tensor", "sv", prev)
                if tb == 0:
                    p.wait("tensor", "sv", fold_done[ec])
                for dc in range(NDC):
                    fn = (lambda e, b=bank, fs=fslot, dc=dc, tb=tb:
                          e.matmul(b[:], fs[:, dc * 128:(dc + 1) * 128],
                                   x_fm[:, dc * NSEQ + tb * 512: dc * NSEQ + (tb + 1) * 512],
                                   start=(dc == 0), stop=(dc == NDC - 1)))
                    if dc < NDC - 1:
                        p.traw(fn)
                    else:
                        mm_done[ec] = p.tclose(fn)
                p.wait("vector", "st", mm_done[ec])
                if it == 0:
                    p.wait("vector", "dl", dl_cos)
                cos_sl = cosk_sb[:, tb * 512:(tb + 1) * 512]
                dstq = out_fm[:, ec * out_stride + tb * 512: ec * out_stride + (tb + 1) * 512]
                ev[it] = p.op("vector", lambda e, b=bank, c=cos_sl, d=dstq:
                              e.tensor_tensor(out=d, in0=b[:], in1=c, op=ALU.mult))
        return ev, fold_done, mm_done

    q_ev, q_folds, q_mms = proj_stream(H, wq_d, q_fm, TQ, TQ // 512, [0, 0])
    k_ev, k_folds, k_mms = proj_stream(HKV, wk_d, k_fm, NSEQ, NSEQ // 512, q_ev[-2:],
                                       prev_fold_tail=q_folds[-2:],
                                       prev_mm_tail=q_mms[-2:])

    p.barrier()
    vslots = [pA[:, 0:512], pA[:, 512:1024], pB[:, 0:512], pB[:, 512:1024],
              pC[:], pD[:], pT32[:, 0:512], pT32[:, 512:1024]]
    v_ev = {}
    wv_mm_last = 0
    for tkg in range(2):
        for half in range(2):
            src = wv_d[half * 1024:(half + 1) * 1024, :].rearrange("(dc q) j -> q dc j", q=128)
            dstv = wv_half[:].rearrange("q (dc j) -> q dc j", dc=8)
            if wv_mm_last:
                p.wait("gpsimd", "st", wv_mm_last)
            p.dma("gpsimd", "dxc", lambda e, s=src, d=dstv: e.dma_start(out=d, in_=s))
            dxc_t = p.cnt["dxc"]
            p.wait("tensor", "dxc", dxc_t)
            for tki in range(8):
                tkc = tkg * 8 + tki
                if half == 0 and tkg > 0:
                    p.wait("tensor", "sa", v_ev[tki])
                for dc8 in range(8):
                    dc = half * 8 + dc8
                    fn = (lambda e, b=vslots[tki], dc=dc, dc8=dc8, tkc=tkc:
                          e.matmul(b,
                                   x_fm[:, dc * NSEQ + tkc * 128: dc * NSEQ + tkc * 128 + 128],
                                   wv_half[:, dc8 * 512:(dc8 + 1) * 512],
                                   start=(dc == 0), stop=(dc == 15)))
                    if not (tki == 7 and dc8 == 7):
                        p.traw(fn)
                    else:
                        wv_mm_last = p.tclose(fn)
            pump_tasks(2)
        p.wait("scalar", "st", wv_mm_last)
        for tki in range(8):
            tkc = tkg * 8 + tki
            v_ev[tki] = p.op("scalar", lambda e, b=vslots[tki], tkc=tkc:
                             e.activation(out=v_tok[:, tkc * 512:(tkc + 1) * 512],
                                          in_=b, func=AF.Copy))

    tap("q_fm", q_fm[:], [128, H * TQ], BF16)
    tap("k_fm", k_fm[:], [128, HKV * NSEQ], BF16)
    tap("v_tok", v_tok[:], [128, NKC * 512], BF16)

    p.barrier()
    pool.release(gX)
    w2stg = sb("w2stg", [128, II], BF16)

    gP = []
    sbP = lambda n, s, d: (gP.append(pool.alloc(n, s, d)) or gP[-1])
    qsq = [sbP(f"qsq{i}", [128, NSEQ], BF16) for i in range(2)]
    brow_bf = sbP("brow_bf", [1, 16], BF16)
    attnout = pool.alloc("attnout", [128, H * TQ], BF16)
    pt = sbP("pt", [128, NKC * 512], BF16)
    rinv = sbP("rinv", [1, 512], F32)
    bc_sb = sbP("bc_sb", [128, 512], F32)

    red_done = 0
    mm_g = 0
    mm_g_ring = [0, 0]
    for ec in range(H + HKV):
        is_q = ec < H
        width = TQ if is_q else NSEQ
        src_fm = q_fm if is_q else k_fm
        off = ec * TQ if is_q else (ec - H) * NSEQ
        qs_slot = qsq[ec % 2]
        if mm_g_ring[ec % 2]:
            p.wait("scalar", "st", mm_g_ring[ec % 2])
        sq_done = p.op("scalar", lambda e, s=src_fm, o=off, w=width, q=qs_slot:
                       e.activation(out=q[:, 0:w], in_=s[:, o:o + w], func=AF.Square))
        p.wait("tensor", "sa", sq_done)
        if red_done:
            p.wait("tensor", "sv", red_done)
        nblk = width // 512
        for blk in range(nblk):
            dstp = (pA if blk < 2 else pB)[0:1, (blk % 2) * 512:(blk % 2) * 512 + 512]
            fn = (lambda e, d=dstp, blk=blk, q=qs_slot: e.matmul(
                d, onescol_b[:], q[:, blk * 512:(blk + 1) * 512], start=True, stop=True))
            if blk < nblk - 1:
                p.traw(fn)
            else:
                mm_g = p.tclose(fn)
        mm_g_ring[ec % 2] = mm_g
        p.wait("vector", "st", mm_g)
        col = ec if is_q else 32 + (ec - H)
        p.op("vector", lambda e, col=col: e.reduce_max(
            out=qkrow[0:1, col:col + 1], in_=pA[0:1, 0:1024], axis=AX.X))
        if not is_q:
            p.op("vector", lambda e: e.reduce_max(
                out=qkrow[0:1, 40:41], in_=pB[0:1, 0:1024], axis=AX.X))
            red_done = p.op("vector", lambda e, col=col: e.tensor_tensor(
                out=qkrow[0:1, col:col + 1], in0=qkrow[0:1, col:col + 1],
                in1=qkrow[0:1, 40:41], op=ALU.max))
        else:
            red_done = p.cnt["sv"]
    prods = 0
    for h in range(H):
        prods = p.op("vector", lambda e, h=h: e.tensor_tensor(
            out=qkrow[0:1, 16 + h:17 + h], in0=qkrow[0:1, h:h + 1],
            in1=qkrow[0:1, 32 + h // 4:33 + h // 4], op=ALU.mult))
    p.wait("scalar", "sv", prods)
    sq2 = p.op("scalar", lambda e: e.activation(
        out=qkrow[0:1, 48:64], in_=qkrow[0:1, 16:32], func=AF.Sqrt))
    p.wait("vector", "sa", sq2)
    p.op("vector", lambda e: e.tensor_scalar_mul(out=qkrow[0:1, 48:64],
                                                 in0=qkrow[0:1, 48:64],
                                                 scalar1=-SCALE * 1.02))
    brow_done = p.op("vector", lambda e: e.tensor_copy(out=brow_bf[:], in_=qkrow[0:1, 48:64]))
    p.wait("tensor", "sv", brow_done)
    p.wait("tensor", "sv", red_done)
    gbias = p.tclose(lambda e: e.matmul(pC[:, 0:16], onesrow_b[:], brow_bf[:],
                                        start=True, stop=True))
    p.wait("scalar", "st", gbias)
    bias_done = p.op("scalar", lambda e: e.activation(out=bias_bc[:], in_=pC[:, 0:16],
                                                      func=AF.Copy))
    tap("bias_bc", bias_bc[:], [128, H], F32)

    p.barrier()
    exp_cnt = {}
    avsub = {}
    av_banks = [pC[:], pT32[:, 512:1024]]
    av_close = {}
    av_ev = {}
    rv_ev = {}
    bc_mm = {}
    bc_ev = {}
    n_it = 0

    def attn_finish(it):
        qs_f = (it // 2) * TQ + (it % 2) * 512
        p.wait("vector", "st", avsub[(it, 7)])
        rv_ev[it] = p.op("vector", lambda e: e.reciprocal(out=rinv[:], in_=pD[0:1, 0:512]))
        p.wait("tensor", "sv", rv_ev[it])
        if it > 0:
            p.wait("tensor", "sa", bc_ev[it - 1])
        bc_mm[it] = p.tclose(lambda e: e.matmul(pT32[:, 0:512], onesrow_f[:], rinv[:],
                                                start=True, stop=True))
        p.wait("scalar", "st", bc_mm[it])
        if it > 0:
            p.wait("scalar", "sv", av_ev[it - 1])
        bc_ev[it] = p.op("scalar", lambda e: e.activation(out=bc_sb[:], in_=pT32[:, 0:512],
                                                          func=AF.Copy))
        pump_tasks(2)
        p.wait("vector", "sa", bc_ev[it])
        av_ev[it] = p.op("vector", lambda e, qs_f=qs_f, b=av_banks[it % 2]: e.tensor_tensor(
            out=attnout[:, qs_f:qs_f + 512], in0=b, in1=bc_sb[:], op=ALU.mult))

    for h in range(H):
        kv = h // 4
        for qb in range(TQ // 512):
            it = n_it
            n_it += 1
            qs = h * TQ + qb * 512
            st_g = {}
            for g in range(8):
                bank = pA if g % 2 == 0 else pB
                if (it, g - 2) in exp_cnt:
                    p.wait("tensor", "sa", exp_cnt[(it, g - 2)])
                elif it > 0 and g < 2:
                    p.wait("tensor", "sa", exp_cnt[(it - 1, 6 + g)])
                for ci in range(2):
                    c = 2 * g + ci
                    fn = (lambda e, b=bank, kv=kv, c=c, ci=ci, qs=qs:
                          e.matmul(b[:, ci * 512:(ci + 1) * 512],
                                   k_fm[:, kv * NSEQ + c * 128: kv * NSEQ + c * 128 + 128],
                                   q_fm[:, qs:qs + 512], start=True, stop=True))
                    if ci == 0:
                        p.traw(fn)
                    else:
                        st_g[g] = p.tclose(fn)
                p.wait("scalar", "st", st_g[g])
                if it > 0:
                    p.wait("scalar", "st", avsub[(it - 1, g)])
                if it == 0 and g == 0:
                    p.wait("scalar", "sa", bias_done)
                exp_cnt[(it, g)] = p.op("scalar", lambda e, b=bank, g=g, h=h: e.activation(
                    out=pt[:, 2 * g * 512: 2 * g * 512 + 1024], in_=b[:],
                    func=AF.Exp, bias=bias_bc[:, h:h + 1], scale=SCALE))
            if it > 0:
                attn_finish(it - 1)
            avb = av_banks[it % 2]
            for g in range(8):
                p.wait("tensor", "sa", exp_cnt[(it, g)])
                if g == 0:
                    if it >= 2:
                        p.wait("tensor", "sv", av_ev[it - 2])
                    if it > 0:
                        p.wait("tensor", "sv", rv_ev[it - 1])
                for ci in range(2):
                    c = 2 * g + ci
                    p.traw(lambda e, kv=kv, c=c, avb=avb: e.matmul(
                        avb,
                        v_tok[:, c * 512 + kv * 128: c * 512 + kv * 128 + 128],
                        pt[:, c * 512:(c + 1) * 512],
                        start=(c == 0), stop=(c == NKC - 1)))
                    fn = (lambda e, c=c: e.matmul(
                        pD[0:1, 0:512], onescol_b[:], pt[:, c * 512:(c + 1) * 512],
                        start=(c == 0), stop=(c == NKC - 1)))
                    if ci < 1:
                        p.traw(fn)
                    else:
                        avsub[(it, g)] = p.tclose(fn)
            if it > 0:
                pump_tasks(1)
    attn_finish(n_it - 1)

    tap("attnout", attnout[:], [128, H * TQ], BF16)
    pump_tasks(208)
    assert tk["done"] == 104
    p.barrier()
    pool.release(gP)
    pool.release(gQKV)
    pool.release([tstg[0], tstg[1], w2stg])

    x1 = sb("x1", [128, NTC * D], BF16)
    gWO = []
    sbW = lambda n, s, d: (gWO.append(pool.alloc(n, s, d)) or gWO[-1])
    x1pre = sbW("x1pre", [128, NTC * D], F32)
    wo_col = [sbW(f"wo_col{i}", [128, H * 512], BF16) for i in range(2)]
    xown = [sbW(f"xown{i}", [128, D], F32) for i in range(2)]

    wo_ev = {}
    xown_user = [0, 0]
    wo_mm_close = {}
    for db in range(4):
        src = wo_d[:, db * 512:(db + 1) * 512].rearrange("(ec q) j -> q ec j", q=128)
        dstv = wo_col[db % 2][:].rearrange("q (ec j) -> q ec j", ec=H)
        if db >= 2:
            p.wait("gpsimd", "st", wo_mm_close[db - 2])
        p.dma("gpsimd", "dxc", lambda e, s=src, d=dstv: e.dma_start(out=d, in_=s))
        dxc_t = p.cnt["dxc"]
        p.wait("tensor", "dxc", dxc_t)
        for tc in range(NTC):
            sl = xown[tc % 2]
            p.wait("sync", "sv", xown_user[tc % 2])
            p.dma("sync", "dl", lambda e, sl=sl, tc=tc:
                  e.dma_start(out=sl[:], in_=xb[tc * 128:(tc + 1) * 128, :]))
            dl_t = p.cnt["dl"]
            bank = pC if tc % 2 == 0 else pD
            it = db * NTC + tc
            if it >= 2:
                pit = it - 2
                p.wait("tensor", "sv", wo_ev[(pit // NTC, pit % NTC)])
            for ec in range(H):
                fn = (lambda e, b=bank, ec=ec, tc=tc, w=wo_col[db % 2]:
                      e.matmul(b[:], attnout[:, ec * TQ + tc * 128: ec * TQ + tc * 128 + 128],
                               w[:, ec * 512:(ec + 1) * 512],
                               start=(ec == 0), stop=(ec == H - 1)))
                if ec < H - 1:
                    p.traw(fn)
                else:
                    wo_mm_close[db] = p.tclose(fn)
            p.wait("vector", "st", wo_mm_close[db])
            p.wait("vector", "dl", dl_t)
            dst = x1pre[:, tc * D + db * 512: tc * D + (db + 1) * 512]
            wo_ev[(db, tc)] = p.op("vector", lambda e, b=bank, sl=sl, db=db, d=dst:
                                   e.tensor_tensor(out=d, in0=b[:],
                                                   in1=sl[:, db * 512:(db + 1) * 512],
                                                   op=ALU.add))
            xown_user[tc % 2] = wo_ev[(db, tc)]
            if db == 3:
                for blk in range(4):
                    p.op("vector", lambda e, tc=tc, blk=blk: e.bn_stats(
                        out=bnst[:, 6 * blk:6 * blk + 6],
                        in_=x1pre[:, tc * D + 512 * blk: tc * D + 512 * (blk + 1)]))
                p.op("vector", lambda e, tc=tc: e.bn_aggr(
                    out=bnag8[:, 2 * tc:2 * tc + 2], in_=bnst[:]))

    def layernorm(src_ap, dst_ap):
        for blk in range(4):
            p.op("vector", lambda e, blk=blk: e.bn_stats(
                out=bnst[:, 6 * blk:6 * blk + 6], in_=src_ap[:, 512 * blk:512 * (blk + 1)]))
        agg = p.op("vector", lambda e: e.bn_aggr(out=bnag[:], in_=bnst[:]))
        p.wait("scalar", "sv", agg)
        sqv = p.op("scalar", lambda e: e.activation(out=colp[:, 0:1], in_=bnag[:, 1:2],
                                                    func=AF.Sqrt, bias=epscol[:, 0:1]))
        p.wait("vector", "sa", sqv)
        p.op("vector", lambda e: e.reciprocal(out=colp[:, 1:2], in_=colp[:, 0:1]))
        nmr = p.op("vector", lambda e: e.scalar_tensor_tensor(
            out=colp[:, 2:3], in0=bnag[:, 0:1], scalar=-1.0, in1=colp[:, 1:2],
            op0=ALU.mult, op1=ALU.mult))
        p.wait("scalar", "sv", nmr)
        return p.op("scalar", lambda e: e.activation(out=dst_ap, in_=src_ap,
                                                     func=AF.Identity,
                                                     scale=colp[:, 1:2], bias=colp[:, 2:3]))

    agg8 = p.cnt["sv"]
    p.wait("scalar", "sv", agg8)
    var8 = bnag8[:].rearrange("q (a b) -> q a b", b=2)[:, :, 1:2]
    mean8 = bnag8[:].rearrange("q (a b) -> q a b", b=2)[:, :, 0:1]
    sq8 = p.op("scalar", lambda e: e.activation(
        out=lncol[:, 0:8].rearrange("q (a b) -> q a b", b=1), in_=var8,
        func=AF.Sqrt, bias=epscol[:, 0:1]))
    p.wait("vector", "sa", sq8)
    p.op("vector", lambda e: e.reciprocal(out=lncol[:, 8:16], in_=lncol[:, 0:8]))
    nmr8 = p.op("vector", lambda e: e.scalar_tensor_tensor(
        out=lncol[:, 16:24].rearrange("q (a b) -> q a b", b=1), in0=mean8, scalar=-1.0,
        in1=lncol[:, 8:16].rearrange("q (a b) -> q a b", b=1),
        op0=ALU.mult, op1=ALU.mult))
    p.wait("scalar", "sv", nmr8)
    ln1_done = [0] * NTC
    for tc in range(NTC):
        ln1_done[tc] = p.op("scalar", lambda e, tc=tc: e.activation(
            out=x1[:, tc * D:(tc + 1) * D], in_=x1pre[:, tc * D:(tc + 1) * D],
            func=AF.Identity, scale=lncol[:, 8 + tc:9 + tc], bias=lncol[:, 16 + tc:17 + tc]))

    tap("x1", x1[:], [128, NTC * D], BF16)
    p.barrier()
    pool.release([attnout])
    pool.release(gWO)

    gXQ = []
    sbq = lambda n, s, d: (gXQ.append(pool.alloc(n, s, d)) or gXQ[-1])
    xq_fm = sbq("xq_fm", [128, NDC * TQ], BF16)
    xq_tok = sbq("xq_tok", [128, D], BF16)
    sxrow = sbq("sxrow", [1, TQ], F32)
    sxbc = sbq("sxbc", [128, TQ], F32)

    h_fm = pool.alloc("h_fm", [128, NIC * TQ], BF16)
    gF1 = []
    sbf = lambda n, s, d: (gF1.append(pool.alloc(n, s, d)) or gF1[-1])
    stage = [sbf(f"stage{i}", [128, D], BF16) for i in range(2)]
    w1t = [sbf(f"w1t{i}", [128, D], BF16) for i in range(2)]
    g1s = sbf("g1s", [128, 512], F32)
    sils = sbf("sils", [128, 512], F32)
    u1s = sbf("u1s", [128, 512], F32)

    gF2 = []
    sbf2 = lambda n, s, d: (gF2.append(pool.alloc(n, s, d)) or gF2[-1])
    tmax = sbf2("tmax", [128, TQ], BF16)
    tmin = sbf2("tmin", [128, TQ], BF16)

    pt_half_ev = [0, 0]
    w1_fetch = {}
    tr_close = {}
    ffn1_ev = {}

    def ffn1_fetch(osl):
        par = osl % 2
        slots = [stage[par * 2], stage[par * 2 + 1]]
        for gu in range(2):
            sidx = gu * 44 + osl
            dep = tr_close.get((osl - 2, gu), 0)
            if dep:
                p.wait("sync", "st", dep)
            p.wait("sync", "dsc", tk["wr"][("w1", sidx)])
            p.dma("sync", "dl", lambda e, s=sidx, d=slots[gu]: e.dma_start(
                out=d[:], in_=w1q_scr[s * 128:(s + 1) * 128, :]))
            w1_fetch[(osl, gu)] = p.cnt["dl"]

    ffn1_fetch(0)

    xq_tr = [0] * NTC
    xq_ev = [0] * NTC
    for tc in range(NTC):
        p.op("vector", lambda e, tc=tc: e.reduce_max(
            out=acol8[:, tc:tc + 1], in_=x1[:, tc * D:(tc + 1) * D],
            axis=AX.X, apply_absolute_value=True))
    p.op("vector", lambda e: e.tensor_scalar_max(out=acol8[:, 0:8], in0=acol8[:, 0:8],
                                                 scalar1=AMAX_MIN))
    p.op("vector", lambda e: e.reciprocal(out=acol8[:, 8:16], in_=acol8[:, 0:8]))
    icd = p.op("vector", lambda e: e.tensor_scalar_mul(out=acol8[:, 8:16],
                                                       in0=acol8[:, 8:16], scalar1=127.0))
    sxc = p.op("vector", lambda e: e.tensor_scalar_mul(out=acol8[:, 16:24],
                                                       in0=acol8[:, 0:8],
                                                       scalar1=1.0 / 127.0))
    p.wait("scalar", "sv", icd)
    for tc in range(NTC):
        x1t = x1[:, tc * D:(tc + 1) * D]
        if tc >= 1:
            p.wait("scalar", "sv", xq_q[tc - 1] if tc >= 1 else 0)
        tqd = p.op("scalar", lambda e, x1t=x1t, tc=tc: e.activation(
            out=scr_a[:], in_=x1t, func=AF.Identity, scale=acol8[:, 8 + tc:9 + tc],
            bias=mcol[:, 0:1]))
        p.wait("vector", "sa", tqd)
        if tc >= 1:
            p.wait("vector", "st", xq_tr[tc - 1])
        qd = p.op("vector", lambda e: e.tensor_scalar(
            out=xq_tok[:], in0=scr_a[:], scalar1=mcol[:, 0:1], scalar2=None,
            op0=ALU.subtract))
        xq_q = xq_q if tc >= 1 else [0] * NTC
        xq_q[tc] = qd
        p.wait("tensor", "sv", qd)
        if tc >= 1:
            p.wait("tensor", "sv", xq_ev[tc - 1])
        for dc in range(NDC):
            p.traw(lambda e, dc=dc: e.transpose(pT[:, dc * 128:(dc + 1) * 128],
                                                xq_tok[:, dc * 128:(dc + 1) * 128], idb[:]))
        xq_tr[tc] = p.tclose(lambda e, tc=tc: e.transpose(
            pA[0:1, tc * 128:(tc + 1) * 128], acol8[:, 16 + tc:17 + tc], idf[:]))
        p.wait("vector", "st", xq_tr[tc])
        src = pT[:].rearrange("q (dc t) -> q dc t", dc=NDC)
        dst = xq_fm[:].rearrange("q (dc t) -> q dc t", dc=NDC)[:, :, tc * 128:(tc + 1) * 128]
        xq_ev[tc] = p.op("vector", lambda e, s=src, d=dst: e.tensor_copy(out=d, in_=s))
    srow_done = p.op("vector", lambda e: e.tensor_copy(out=sxrow[:], in_=pA[0:1, 0:TQ]))
    p.wait("tensor", "sv", srow_done)
    gb1 = p.tclose(lambda e: e.matmul(pB[:, 0:512], onesrow_f[:], sxrow[0:1, 0:512],
                                      start=True, stop=True))
    gb2 = p.tclose(lambda e: e.matmul(pB[:, 512:1024], onesrow_f[:], sxrow[0:1, 512:1024],
                                      start=True, stop=True))
    p.wait("scalar", "st", gb2)
    sxbc_done = p.op("scalar", lambda e: e.activation(out=sxbc[:], in_=pB[:], func=AF.Copy))

    tap("xq_fm", xq_fm[:], [128, NDC * TQ], BF16)
    tap("sxbc", sxbc[:], [128, TQ], F32)


    xqb_sv = p.cnt["sv"]
    xqb_sa = p.cnt["sa"]
    xqb_st = p.cnt["st"]
    gXQ.remove(xq_tok); gXQ.remove(sxrow)
    pool.release([xq_tok, sxrow])
    stage.extend([pool.alloc(f"stage{i}", [128, D], BF16) for i in (2, 3)])
    gF1.extend(stage[2:4])

    def ffn1_trs(osl):
        stg = [stage[(osl % 2) * 2], stage[(osl % 2) * 2 + 1]]
        for gu in range(2):
            p.wait("tensor", "dl", w1_fetch[(osl, gu)])
            if osl == 0 and gu == 0:
                p.wait("tensor", "sv", xqb_sv)
                p.wait("tensor", "sa", xqb_sa)
            for half in range(2):
                bank = (2 * gu + half) % 2
                hoff = bank * 1024
                if pt_half_ev[bank]:
                    p.wait("tensor", "sa", pt_half_ev[bank])
                for j in range(8):
                    dc = half * 8 + j
                    fn = (lambda e, s=stg[gu], dc=dc, hoff=hoff, j=j: e.transpose(
                        pT[:, hoff + j * 128: hoff + (j + 1) * 128],
                        s[:, dc * 128:(dc + 1) * 128], idb[:]))
                    if j < 7:
                        p.traw(fn)
                    else:
                        tg = p.tclose(fn)
                p.wait("scalar", "st", tg)
                pt_half_ev[bank] = p.op("scalar", lambda e, d=w1t[gu], half=half, hoff=hoff:
                                        e.activation(out=d[:, half * 1024:(half + 1) * 1024],
                                                     in_=pT[:, hoff:hoff + 1024], func=AF.Copy))
            tr_close[(osl, gu)] = tg
        if osl == 0:
            tap("w1t0", w1t[0][:], [128, D], BF16)

    def ffn1_mms(osl):
        for tb in range(2):
            p.wait("tensor", "sa", ffn1_wt_ready[osl])
            prev = ffn1_ev.get((osl - 1, tb), 0)
            if prev:
                p.wait("tensor", "sv", prev)
            gmm = 0
            for which, psx in ((0, pA), (1, pB)):
                for hc in range(NDC):
                    fn = (lambda e, b=psx, w=w1t[which], hc=hc, tb=tb: e.matmul(
                        b[:, tb * 512:(tb + 1) * 512], w[:, hc * 128:(hc + 1) * 128],
                        xq_fm[:, hc * TQ + tb * 512: hc * TQ + (tb + 1) * 512],
                        start=(hc == 0), stop=(hc == NDC - 1)))
                    if not (which == 1 and hc == NDC - 1):
                        p.traw(fn)
                    else:
                        gmm = p.tclose(fn)
            ffn1_gmm[(osl, tb)] = gmm

    def ffn1_epi(osl):
        for tb in range(2):
            p.wait("vector", "st", ffn1_gmm[(osl, tb)])
            if tb == 0 and osl == 0:
                p.wait("vector", "sa", sxbc_done)
            gg = p.op("vector", lambda e, tb=tb: e.tensor_tensor(
                out=g1s[:], in0=pA[:, tb * 512:(tb + 1) * 512],
                in1=sxbc[:, tb * 512:(tb + 1) * 512], op=ALU.mult))
            p.wait("scalar", "sv", gg)
            sil_done = p.op("scalar", lambda e, sw=swall[:, osl:osl + 1]:
                            e.activation(out=sils[:], in_=g1s[:], func=AF.Silu, scale=sw))
            p.op("vector", lambda e, tb=tb: e.tensor_tensor(
                out=u1s[:], in0=pB[:, tb * 512:(tb + 1) * 512],
                in1=sxbc[:, tb * 512:(tb + 1) * 512], op=ALU.mult))
            p.wait("vector", "sa", sil_done)
            ffn1_ev[(osl, tb)] = p.op(
                "vector", lambda e, osl=osl, tb=tb, sw=swall[:, 44 + osl:45 + osl]:
                e.scalar_tensor_tensor(
                    out=h_fm[:, osl * TQ + tb * 512: osl * TQ + (tb + 1) * 512],
                    in0=u1s[:], scalar=sw, in1=sils[:], op0=ALU.mult, op1=ALU.mult))
        if osl == 0:
            p.op("vector", lambda e: e.tensor_copy(out=tmax[:], in_=h_fm[:, 0:TQ]))
            p.op("vector", lambda e: e.tensor_copy(out=tmin[:], in_=h_fm[:, 0:TQ]))
        else:
            p.op("vector", lambda e, osl=osl: e.tensor_tensor(
                out=tmax[:], in0=tmax[:], in1=h_fm[:, osl * TQ:(osl + 1) * TQ], op=ALU.max))
            p.op("vector", lambda e, osl=osl: e.tensor_tensor(
                out=tmin[:], in0=tmin[:], in1=h_fm[:, osl * TQ:(osl + 1) * TQ], op=ALU.min))

    ffn1_gmm = {}
    ffn1_wt_ready = {}
    p.wait("sync", "st", xqb_st)
    p.wait("sync", "sv", xqb_sv)
    ffn1_fetch(1)
    ffn1_trs(0)
    ffn1_wt_ready[0] = max(pt_half_ev)
    tap("w1q0", stage[0][:], [128, D], BF16)
    tap("w1q0u", stage[1][:], [128, D], BF16)
    for osl in range(NIC):
        if osl + 2 < NIC:
            ffn1_fetch(osl + 2)
        ffn1_mms(osl)
        if osl + 1 < NIC:
            ffn1_trs(osl + 1)
            ffn1_wt_ready[osl + 1] = max(pt_half_ev)
        ffn1_epi(osl)

    tap("h_fm", h_fm[:], [128, NIC * TQ], BF16)
    p.barrier()
    pool.release(gXQ)
    pool.release(gF1)

    hrow = sbf2("hrow", [1, TQ], F32)
    irow = sbf2("irow", [1, TQ], F32)
    i127bc = sbf2("i127bc", [128, TQ], BF16)
    shbc = sbf2("shbc", [128, TQ], F32)
    tm_done = p.cnt["sv"]
    p.wait("tensor", "sv", tm_done)
    for tc in range(NTC):
        p.traw(lambda e, tc=tc: e.transpose(
            pT[:, tc * 128:(tc + 1) * 128], tmax[:, tc * 128:(tc + 1) * 128], idb[:]))
        fn = (lambda e, tc=tc: e.transpose(
            pT[:, 1024 + tc * 128: 1024 + (tc + 1) * 128],
            tmin[:, tc * 128:(tc + 1) * 128], idb[:]))
        if tc < NTC - 1:
            p.traw(fn)
        else:
            t3g = p.tclose(fn)
    p.wait("vector", "st", t3g)
    micro = 0
    for tc in range(NTC):
        if micro:
            p.wait("vector", "st", micro)
        p.op("vector", lambda e, tc=tc: e.reduce_max(
            out=colp[:, 5:6], in_=pT[:, tc * 128:(tc + 1) * 128], axis=AX.X,
            apply_absolute_value=True))
        p.op("vector", lambda e, tc=tc: e.reduce_max(
            out=colp[:, 6:7], in_=pT[:, 1024 + tc * 128: 1024 + (tc + 1) * 128], axis=AX.X,
            apply_absolute_value=True))
        p.op("vector", lambda e: e.tensor_tensor(out=colp[:, 5:6], in0=colp[:, 5:6],
                                                 in1=colp[:, 6:7], op=ALU.max))
        p.op("vector", lambda e: e.tensor_scalar_max(out=colp[:, 5:6], in0=colp[:, 5:6],
                                                     scalar1=AMAX_MIN))
        amc = p.op("vector", lambda e: e.tensor_copy(out=amaxf[:], in_=colp[:, 5:6]))
        p.wait("tensor", "sv", amc)
        micro = p.tclose(lambda e, tc=tc: e.transpose(
            pA[0:1, tc * 128:(tc + 1) * 128], amaxf[:], idf[:]))
    p.wait("vector", "st", micro)
    p.op("vector", lambda e: e.tensor_copy(out=hrow[:], in_=pA[0:1, 0:TQ]))
    p.op("vector", lambda e: e.reciprocal(out=irow[:], in_=hrow[:]))
    ird = p.op("vector", lambda e: e.tensor_scalar_mul(out=irow[:], in0=irow[:],
                                                       scalar1=127.0))
    shd = p.op("vector", lambda e: e.tensor_scalar_mul(out=hrow[:], in0=hrow[:],
                                                       scalar1=1.0 / 127.0))
    p.wait("tensor", "sv", ird)
    gb1 = p.tclose(lambda e: e.matmul(pB[:, 0:512], onesrow_f[:], irow[0:1, 0:512],
                                      start=True, stop=True))
    gb2 = p.tclose(lambda e: e.matmul(pB[:, 512:1024], onesrow_f[:], irow[0:1, 512:1024],
                                      start=True, stop=True))
    p.wait("scalar", "st", gb2)
    ibc_done = p.op("scalar", lambda e: e.activation(out=i127bc[:], in_=pB[:], func=AF.Copy))
    p.wait("tensor", "sa", ibc_done)
    p.wait("tensor", "sv", shd)
    gb3 = p.tclose(lambda e: e.matmul(pB[:, 0:512], onesrow_f[:], hrow[0:1, 0:512],
                                      start=True, stop=True))
    gb4 = p.tclose(lambda e: e.matmul(pB[:, 512:1024], onesrow_f[:], hrow[0:1, 512:1024],
                                      start=True, stop=True))
    p.wait("scalar", "st", gb4)
    shbc_done = p.op("scalar", lambda e: e.activation(out=shbc[:], in_=pB[:], func=AF.Copy))

    gF2.remove(tmax); gF2.remove(tmin); gF2.remove(hrow); gF2.remove(irow)
    pool.release([tmax, tmin, hrow, irow])
    p.barrier()
    gF3 = []
    sbf3 = lambda n, s, d: (gF3.append(pool.alloc(n, s, d)) or gF3[-1])
    outfm = pool.alloc("outfm", [128, NDC * TQ], BF16)
    stage2 = [sbf3("stage2_0", [128, II], BF16)]
    w2t = sbf3("w2t", [128, II], BF16)
    o1s = sbf3("o1s", [128, 512], F32)
    pt2_half_ev = list(pt_half_ev)
    w2_fetch = {}
    w2_tr_last = {}
    ffn2_ev = {}

    def ffn2_fetch(oc):
        dep = w2_tr_last.get(oc - 1, 0)
        if dep:
            p.wait("sync", "st", dep)
        p.wait("sync", "dsc", tk["wr"][("w2", oc)])
        p.dma("sync", "dl", lambda e, oc=oc, s=stage2[0]: e.dma_start(
            out=s[:], in_=w2q_scr[oc * 128:(oc + 1) * 128, :]))
        w2_fetch[oc] = p.cnt["dl"]

    def ffn2_trs(oc):
        for batch in range(6):
            b0 = batch * 8
            bn = 8 if batch < 5 else 4
            bank = batch % 2
            hoff = bank * 1024
            if pt2_half_ev[bank]:
                p.wait("tensor", "sa", pt2_half_ev[bank])
            if batch == 0:
                p.wait("tensor", "dl", w2_fetch[oc])
            for j in range(bn):
                icx = b0 + j
                fn = (lambda e, icx=icx, j=j, hoff=hoff, s=stage2[0]: e.transpose(
                    pT[:, hoff + j * 128: hoff + (j + 1) * 128],
                    s[:, icx * 128:(icx + 1) * 128], idb[:]))
                if j < bn - 1:
                    p.traw(fn)
                else:
                    tg = p.tclose(fn)
            p.wait("scalar", "st", tg)
            pt2_half_ev[bank] = p.op("scalar", lambda e, b0=b0, bn=bn, hoff=hoff:
                                     e.activation(out=w2t[:, b0 * 128:(b0 + bn) * 128],
                                                  in_=pT[:, hoff:hoff + bn * 128],
                                                  func=AF.Copy))
            w2_tr_last[oc] = tg
        ffn2_wt_ready[oc] = max(pt2_half_ev)
        if oc == 0:
            tap("w2t0", w2t[:], [128, II], BF16)

    def ffn2_mms(oc):
        for tb in range(2):
            p.wait("tensor", "sa", ffn2_wt_ready[oc])
            if oc == 0 and tb == 1:
                p.wait("tensor", "sv", hqb_sv)
            prev = ffn2_ev.get((oc - 1, tb), 0)
            if prev:
                p.wait("tensor", "sv", prev)
            for icx in range(NIC):
                if oc == 0 and tb == 0 and icx % 11 == 0:
                    p.wait("tensor", "sv", hq_done[min(icx + 10, NIC - 1)])
                fn = (lambda e, icx=icx, tb=tb: e.matmul(
                    pA[:, tb * 512:(tb + 1) * 512], w2t[:, icx * 128:(icx + 1) * 128],
                    h_fm[:, icx * TQ + tb * 512: icx * TQ + (tb + 1) * 512],
                    start=(icx == 0), stop=(icx == NIC - 1)))
                if icx < NIC - 1:
                    p.traw(fn)
                else:
                    gmm = p.tclose(fn)
            ffn2_gmm[(oc, tb)] = gmm

    def ffn2_epi(oc):
        for tb in range(2):
            p.wait("vector", "st", ffn2_gmm[(oc, tb)])
            if oc == 0 and tb == 0:
                p.wait("vector", "sa", shbc_done)
            p.op("vector", lambda e, tb=tb: e.tensor_tensor(
                out=o1s[:], in0=pA[:, tb * 512:(tb + 1) * 512],
                in1=shbc[:, tb * 512:(tb + 1) * 512], op=ALU.mult))
            ffn2_ev[(oc, tb)] = p.op(
                "vector", lambda e, oc=oc, tb=tb, sw=swall[:, 88 + oc:89 + oc]:
                e.tensor_scalar(
                    out=outfm[:, oc * TQ + tb * 512: oc * TQ + (tb + 1) * 512],
                    in0=o1s[:], scalar1=sw, scalar2=None, op0=ALU.mult))

    ffn2_gmm = {}
    ffn2_wt_ready = {}
    ffn2_fetch(0)

    hq_done = [0] * NIC
    p.wait("vector", "sa", ibc_done)
    for ic in range(NIC):
        p.op("vector", lambda e, ic=ic: e.tensor_tensor(
            out=scr_a[:, 0:TQ], in0=h_fm[:, ic * TQ:(ic + 1) * TQ], in1=i127bc[:],
            op=ALU.mult))
        hq_done[ic] = p.op("vector", lambda e, ic=ic: e.tensor_scalar(
            out=h_fm[:, ic * TQ:(ic + 1) * TQ], in0=scr_a[:, 0:TQ],
            scalar1=mcol[:, 0:1], scalar2=mcol[:, 0:1], op0=ALU.add, op1=ALU.subtract))

    tap("hq_fm", h_fm[:], [128, NIC * TQ], BF16)
    tap("shbc", shbc[:], [128, TQ], F32)

    hqb_sv = p.cnt["sv"]

    ffn2_trs(0)
    for oc in range(NDC):
        if oc + 1 < NDC:
            ffn2_fetch(oc + 1)
        ffn2_mms(oc)
        if oc + 1 < NDC:
            ffn2_trs(oc + 1)
        ffn2_epi(oc)

    tap("outfm", outfm[:], [128, NDC * TQ], BF16)
    p.barrier()
    pool.release([h_fm])
    pool.release(gF2)
    pool.release(gF3)

    yln8 = pool.alloc("yln8", [128, NTC * D], F32)
    fin_tr = [0] * NTC
    fin_y = [0] * NTC
    st_out = [0] * NTC
    for tc in range(NTC):
        if tc >= 1:
            p.wait("tensor", "sv", fin_y[tc - 1])
        for oc in range(NDC):
            fn = (lambda e, oc=oc, tc=tc: e.transpose(
                pT[:, oc * 128:(oc + 1) * 128],
                outfm[:, oc * TQ + tc * 128: oc * TQ + tc * 128 + 128], idb[:]))
            if oc < NDC - 1:
                p.traw(fn)
            else:
                fin_tr[tc] = p.tclose(fn)
        p.wait("vector", "st", fin_tr[tc])
        fin_y[tc] = p.op("vector", lambda e, tc=tc: e.scalar_tensor_tensor(
            out=yln8[:, tc * D:(tc + 1) * D], in0=pT[:], scalar=1.0,
            in1=x1[:, tc * D:(tc + 1) * D], op0=ALU.mult, op1=ALU.add))
        for blk in range(4):
            p.op("vector", lambda e, tc=tc, blk=blk: e.bn_stats(
                out=bnst[:, 6 * blk:6 * blk + 6],
                in_=yln8[:, tc * D + 512 * blk: tc * D + 512 * (blk + 1)]))
        p.op("vector", lambda e, tc=tc: e.bn_aggr(
            out=bnag8[:, 2 * tc:2 * tc + 2], in_=bnst[:]))
    agg8b = p.cnt["sv"]
    p.wait("scalar", "sv", agg8b)
    var8b = bnag8[:].rearrange("q (a b) -> q a b", b=2)[:, :, 1:2]
    mean8b = bnag8[:].rearrange("q (a b) -> q a b", b=2)[:, :, 0:1]
    sq8b = p.op("scalar", lambda e: e.activation(
        out=lncol[:, 0:8].rearrange("q (a b) -> q a b", b=1), in_=var8b,
        func=AF.Sqrt, bias=epscol[:, 0:1]))
    p.wait("vector", "sa", sq8b)
    p.op("vector", lambda e: e.reciprocal(out=lncol[:, 8:16], in_=lncol[:, 0:8]))
    nmr8b = p.op("vector", lambda e: e.scalar_tensor_tensor(
        out=lncol[:, 16:24].rearrange("q (a b) -> q a b", b=1), in0=mean8b, scalar=-1.0,
        in1=lncol[:, 8:16].rearrange("q (a b) -> q a b", b=1),
        op0=ALU.mult, op1=ALU.mult))
    p.wait("scalar", "sv", nmr8b)
    for tc in range(NTC):
        ln2 = p.op("scalar", lambda e, tc=tc: e.activation(
            out=yln8[:, tc * D:(tc + 1) * D], in_=yln8[:, tc * D:(tc + 1) * D],
            func=AF.Identity, scale=lncol[:, 8 + tc:9 + tc], bias=lncol[:, 16 + tc:17 + tc]))
        p.wait("sync", "sa", ln2)
        st_out[tc] = p.dma("sync", "dso", lambda e, tc=tc: e.dma_start(
            out=out_d[tc * 128:(tc + 1) * 128, :], in_=yln8[:, tc * D:(tc + 1) * D]))
    p.wait("sync", "dso", p.cnt["dso"])

    sems = {}
    semctx = ExitStack()
    for s in ALL_SEMS:
        sems[s] = semctx.enter_context(nc.semaphore(s))

    def run_stream(eng, items):
        for it in items:
            if it[0] == "w":
                eng.wait_ge(sems[it[1]], it[2])
            elif it[0] == "i":
                inst = it[1](eng)
                inst.then_inc(sems[it[2]], it[3])
            else:
                it[1](eng)

    pool.finalize(ctx)
    with ctx, semctx, nc.Block() as block:
        block.sync(lambda e: run_stream(e, p.stream["sync"]))
        block.scalar(lambda e: run_stream(e, p.stream["scalar"]))
        block.vector(lambda e: run_stream(e, p.stream["vector"]))
        block.tensor(lambda e: run_stream(e, p.stream["tensor"]))
        block.gpsimd(lambda e: run_stream(e, p.stream["gpsimd"]))
    return nc


def _cos_table():
    j = np.arange(0, HD, 2, dtype=np.float64)
    inv_freq = 1.0 / (10000.0 ** (j / HD))
    t = np.arange(NSEQ, dtype=np.float64)
    freqs = t[:, None] * inv_freq[None, :]
    emb = np.concatenate([freqs, freqs], axis=-1)
    return np.cos(emb).T.astype(np.float32)


def _consts():
    return {
        "idb": np.eye(128, dtype=ml_dtypes.bfloat16),
        "idf": np.eye(128, dtype=np.float32),
        "onescol_b": np.ones((128, 1), dtype=ml_dtypes.bfloat16),
        "onesrow_f": np.ones((1, 128), dtype=np.float32),
        "onesrow_b": np.ones((1, 128), dtype=ml_dtypes.bfloat16),
    }


def make_in_maps(x, wq, wk, wv, wo, w1, w2):
    cos = _cos_table()
    consts = _consts()
    in_maps = []
    for core in range(8):
        b, half = core // 2, core % 2
        own = slice(half * TQ, (half + 1) * TQ)
        oth = slice((1 - half) * TQ, (2 - half) * TQ)
        xbp = np.ascontiguousarray(np.concatenate([x[b][own], x[b][oth]], axis=0))
        coskp = np.ascontiguousarray(np.concatenate([cos[:, own], cos[:, oth]], axis=1))
        m = {"xb": xbp, "cosk": coskp, "wq": wq, "wk": wk, "wv": wv, "wo": wo,
             "w1": w1, "w2": w2}
        m.update(consts)
        in_maps.append(m)
    return in_maps


def kernel(L0_attn_x, L0_attn_Wq, L0_attn_Wk, L0_attn_Wv, L0_attn_Wo,
           L0_ffn_w_gate_up, L0_ffn_w_down):
    x = np.ascontiguousarray(np.asarray(L0_attn_x, dtype=np.float32))
    wq = np.ascontiguousarray(np.asarray(L0_attn_Wq, dtype=np.float32))
    wk = np.ascontiguousarray(np.asarray(L0_attn_Wk, dtype=np.float32))
    wv = np.ascontiguousarray(np.asarray(L0_attn_Wv, dtype=np.float32))
    wo = np.ascontiguousarray(np.asarray(L0_attn_Wo, dtype=np.float32))
    w1 = np.ascontiguousarray(np.asarray(L0_ffn_w_gate_up, dtype=np.float32))
    w2 = np.ascontiguousarray(np.asarray(L0_ffn_w_down, dtype=np.float32))

    nc = bass.Bass("TRN2", target_bir_lowering=False)
    build_program(nc)
    in_maps = make_in_maps(x, wq, wk, wv, wo, w1, w2)
    res = run_bass_kernel_spmd(nc, in_maps, core_ids=list(range(8)))

    out = np.empty((4, NSEQ, D), dtype=np.float32)
    for core in range(8):
        b, half = core // 2, core % 2
        out[b, half * TQ:(half + 1) * TQ] = res.results[core]["out"]
    return out
